# version 17
# baseline (speedup 1.0000x reference)
import numpy as np
import ml_dtypes
import concourse.bass as bass
import concourse.mybir as mybir
from concourse.bass_utils import run_bass_kernel_spmd

F32 = mybir.dt.float32
BF16 = mybir.dt.bfloat16
I32 = mybir.dt.int32
AF = mybir.ActivationFunctionType
ALU = mybir.AluOpType
AX = mybir.AxisListType.X

S = 2048
D = 1024
INNER = 4096
NT = S // 128
NCH = D // 128
NB = S // 256
PI = float(np.float32(np.pi))
TWO_PI = float(np.float32(2.0 * np.pi))
HALF_PI = float(np.float32(np.pi / 2.0))

ENGINES = ("sync", "gpsimd", "scalar", "vector", "tensor")
SEMNAMES = ("dS", "dG", "SA0", "SA1", "TG0", "TG1", "TD0", "TD1", "XR", "XQ",
            "GA0", "GA1", "GO", "RT0", "RT1", "RT2", "RT3", "T", "V", "SC", "GP",
            "SW0", "SW1", "SW2", "SW3", "GW0", "GW1")
DTSIZE = {F32: 4, BF16: 2, I32: 4}


class Prog:
    def __init__(self):
        self.streams = {e: [] for e in ENGINES}
        self.tot = {s: 0 for s in SEMNAMES}

    def val(self, sem):
        return self.tot[sem]

    def emit(self, eng, build, sem=None, n=1, waits=()):
        wv = []
        for w in waits:
            if isinstance(w, tuple):
                s, v = w
                wv.append((s, self.tot[s] if v is None else v))
            else:
                wv.append((w, self.tot[w]))
        if sem is not None:
            self.tot[sem] += n

        def fn(e, SEMS, waited):
            for s, v in wv:
                if v > 0 and waited.get(s, 0) < v:
                    e.wait_ge(SEMS[s], v)
                    waited[s] = v
            ins = build(e)
            if sem is not None and ins is not None:
                ins.then_inc(SEMS[sem], n)

        self.streams[eng].append(fn)

    def barrier(self):
        snap = dict(self.tot)
        owner = {"dS": "sync", "dG": "gpsimd", "GP": "gpsimd",
                 "SA0": "sync", "SA1": "sync", "TG0": "sync", "TG1": "sync",
                 "TD0": "sync", "TD1": "sync", "XR": "sync", "XQ": "sync",
                 "GA0": "gpsimd", "GA1": "gpsimd", "GO": "gpsimd",
                 "RT0": "gpsimd", "RT1": "gpsimd", "RT2": "gpsimd", "RT3": "gpsimd",
                 "SW0": "sync", "SW1": "sync", "SW2": "sync", "SW3": "sync",
                 "GW0": "gpsimd", "GW1": "gpsimd",
                 "T": "tensor", "V": "vector", "SC": "scalar"}
        for eng in ENGINES:
            waits = [(s, snap[s]) for s in SEMNAMES if owner[s] != eng and snap[s] > 0]
            self.emit(eng, lambda e: None, waits=waits)

    def install(self, nc, SEMS, final_waits):
        with nc.Block() as blk:
            for eng in ENGINES:
                fns = list(self.streams[eng])

                def make(fns, eng):
                    def body(e):
                        waited = {}
                        for fn in fns:
                            fn(e, SEMS, waited)
                        if eng == "sync":
                            for s, v in final_waits:
                                e.wait_ge(SEMS[s], v)
                    return body

                getattr(blk, eng)(make(fns, eng))


def build_program(phases="TWABC"):
    import os
    A1N = int(os.environ.get("A1N", str(NT)))
    A1SKIP = set(os.environ.get("A1SKIP", "").split(","))
    CN = int(os.environ.get("CN", "4"))
    CSKIP = set(os.environ.get("CSKIP", "").split(","))
    nc = bass.Bass("TRN2", target_bir_lowering=False)
    p = Prog()

    def emit(eng, b, **kw):
        p.emit(eng, b, **kw)

    x_d = nc.dram_tensor("x", [S, D], F32, kind="ExternalInput")
    wqkv_d = nc.dram_tensor("wqkv", [3 * D, D], F32, kind="ExternalInput")
    wproj_d = nc.dram_tensor("wproj", [D, D], F32, kind="ExternalInput")
    bproj_d = nc.dram_tensor("bproj", [1, D], F32, kind="ExternalInput")
    wgT_d = nc.dram_tensor("wgT", [D, INNER], F32, kind="ExternalInput")
    wuT_d = nc.dram_tensor("wuT", [D, INNER], F32, kind="ExternalInput")
    wdT_d = nc.dram_tensor("wdT", [INNER, D], F32, kind="ExternalInput")
    invf_hi_d = nc.dram_tensor("invf_hi", [32, 1], F32, kind="ExternalInput")
    invf_lo_d = nc.dram_tensor("invf_lo", [32, 1], F32, kind="ExternalInput")
    out_d = nc.dram_tensor("out", [S, D], F32, kind="ExternalOutput")

    SCRK = "ExternalOutput" if os.environ.get("DBG_SCRATCH") else "Internal"
    DBGA1 = bool(os.environ.get("DBG_A1"))
    if DBGA1:
        dbgst_d = nc.dram_tensor("dbgst", [NT * 128, 16], F32, kind="ExternalOutput")
        dbgxn_d = nc.dram_tensor("dbgxn", [NT * 128, D], F32, kind="ExternalOutput")
    DBGC = bool(os.environ.get("DBG_C"))
    if DBGC:
        dbghq_d = nc.dram_tensor("dbghq", [S, INNER], BF16, kind="ExternalOutput")
        dbghst_d = nc.dram_tensor("dbghst", [NT * 128, 48], F32, kind="ExternalOutput")
        dbghqf_d = nc.dram_tensor("dbghqf", [NT * 128, 16384], BF16, kind="ExternalOutput")
    DBGA2 = bool(os.environ.get("DBG_A2"))
    if DBGA2:
        dbgq_d = nc.dram_tensor("dbgq", [128, NCH * S], F32, kind="ExternalOutput")
        dbgk_d = nc.dram_tensor("dbgk", [128, NCH * S], F32, kind="ExternalOutput")
        dbgv_d = nc.dram_tensor("dbgv", [128, NT * D], F32, kind="ExternalOutput")
        dbgo_d = nc.dram_tensor("dbgo", [NB * 128, 8 * 256], F32, kind="ExternalOutput")
    terng_d = nc.dram_tensor("terng_scr", [D, INNER], BF16, kind=SCRK)
    ternu_d = nc.dram_tensor("ternu_scr", [D, INNER], BF16, kind=SCRK)
    ternd_d = nc.dram_tensor("ternd_scr", [INNER, D], BF16, kind=SCRK)
    xqfm_d = nc.dram_tensor("xqfm_scr", [D, S], BF16, kind=SCRK)
    attn_d = nc.dram_tensor("attn_scr", [S, D], BF16, kind=SCRK)

    BASE = 24576
    cursor = [BASE]

    OFFS = {}

    def alloc(name, shape, dtype, at=None):
        nbytes = int(np.prod(shape[1:])) * DTSIZE[dtype]
        off = cursor[0] if at is None else at
        assert off % 32 == 0, (name, off)
        t = nc.alloc_sbuf_tensor_at(name, list(shape), dtype, offset=off)
        OFFS[id(t)] = off
        if at is None:
            cursor[0] = off + ((nbytes + 31) // 32) * 32
        return t

    def offof(t):
        return OFFS[id(t)]

    consts = alloc("consts", [128, 8], F32)
    onesf = alloc("onesf", [128, 1], F32)
    ones_bf = alloc("ones_bf", [128, 64], BF16)
    onesrow_f = alloc("onesrow_f", [1, 128], F32)
    onesrow_bf = alloc("onesrow_bf", [1, 128], BF16)
    ident = alloc("ident", [128, 128], BF16)
    maskAB = alloc("maskAB", [128, 512], F32)
    cosF = alloc("cosF", [128, S], BF16)
    sinS = alloc("sinS", [128, S], BF16)
    cx = alloc("cx", [128, NT], F32)
    crep = alloc("crep", [128, 8], F32)
    stage = alloc("stage", [1, 8], F32)
    bproj_bf = alloc("bproj_bf", [1, D], BF16)
    stats = alloc("stats", [128, 16], F32)
    hstats = alloc("hstats", [128, 56], F32)
    PERS_END = cursor[0]

    cursor[0] = PERS_END
    xnfm = alloc("xnfm", [128, NCH * S], BF16)
    kfm = alloc("kfm", [128, NCH * S], BF16)
    qfm = alloc("qfm", [128, NCH * S], BF16)
    vtm = alloc("vtm", [128, NT * D], BF16)
    wprojb = alloc("wprojb", [128, NCH * D], BF16, at=PERS_END)
    xbuf = [alloc("xbuf0", [128, D], F32), alloc("xbuf1", [128, D], F32)]
    xntm = alloc("xntm", [128, D], F32)
    xqi = alloc("xqi", [128, D], I32)
    xqb = [alloc("xqb0", [128, D], BF16), alloc("xqb1", [128, D], BF16)]
    xnb = [alloc("xnb0", [128, D], BF16), alloc("xnb1", [128, D], BF16)]
    xqstg = [alloc("xqstg0", [128, D], BF16), alloc("xqstg1", [128, D], BF16)]
    ast = alloc("ast", [128, 240], F32, at=offof(xntm))
    wqb = [alloc("wqb0", [128, 1024], F32), alloc("wqb1", [128, 1024], F32)]
    wqbf = [alloc("wqbf0", [128, 1024], BF16), alloc("wqbf1", [128, 1024], BF16)]
    qsb = [alloc("qsb%d" % i, [128, 512], BF16) for i in range(4)]
    rotb = [alloc("rotb%d" % i, [128, 512], BF16) for i in range(4)]
    tP = alloc("tP", [128, 512], BF16)
    tR = alloc("tR", [128, 512], BF16)
    AB_END = cursor[0]
    utab = alloc("utab", [128, S], F32, at=offof(kfm))
    utmp2 = alloc("utmp2", [128, S], F32, at=offof(kfm) + 8192)
    ts1 = alloc("ts1", [128, S], F32, at=offof(kfm) + 16384)
    ts2 = alloc("ts2", [128, S], F32, at=offof(kfm) + 24576)
    utmp = alloc("utmp", [128, 1024], F32, at=offof(qfm))
    bpf = alloc("bpf", [1, D], F32, at=offof(qfm) + 4096)
    fbuf = alloc("fbuf", [128, 1], F32, at=offof(qfm) + 8192)
    flbuf = alloc("flbuf", [128, 1], F32, at=offof(qfm) + 8224)
    wbuf4 = [alloc("wb4_%d" % i, [128, 1024], F32, at=offof(vtm) + 4096 * i) for i in range(4)]
    ws1 = alloc("ws1", [128, 1024], F32, at=offof(vtm) + 16384)
    ws2 = alloc("ws2", [128, 1024], F32, at=offof(vtm) + 20480)
    wternb = [alloc("wternb0", [128, 1024], BF16, at=offof(vtm) + 24576),
              alloc("wternb1", [128, 1024], BF16, at=offof(vtm) + 26624)]
    wsum = alloc("wsum", [128, 96], F32, at=offof(vtm) + 28672)
    acc2 = alloc("acc2", [128, 1], F32, at=offof(vtm) + 29056)
    xsq = alloc("xsq", [128, 1024], F32, at=offof(xqi))
    vtmp = [alloc("vtmp0", [128, 512], BF16, at=offof(xqstg[0])),
            alloc("vtmp1", [128, 512], BF16, at=offof(xqstg[1]))]
    Eb = [alloc("Eb0", [128, 1024], BF16, at=offof(xbuf[0])),
          alloc("Eb1", [128, 1024], BF16, at=offof(xbuf[1]))]
    Ofm = alloc("Ofm", [128, 8 * 256], BF16, at=offof(xntm))
    lnl = alloc("lnl", [128, 256], F32, at=offof(xqi))
    recl = alloc("recl", [128, 256], F32, at=offof(xqi) + 1024)
    attst = [alloc("attst0", [128, 512], BF16, at=offof(xqb[0])),
             alloc("attst1", [128, 512], BF16, at=offof(xnb[0]))]

    cursor[0] = PERS_END
    hB = alloc("hB", [128, 4 * 4096], F32)
    hqfm = alloc("hqfm", [128, 16384], BF16)
    terng = [alloc("terngA", [128, 4096], BF16), alloc("terngB", [128, 4096], BF16)]
    ternu = [alloc("ternuA", [128, 4096], BF16), alloc("ternuB", [128, 4096], BF16)]
    terndP = [alloc("terndA", [128, 2048], BF16), alloc("terndB", [128, 2048], BF16)]
    xqf = alloc("xqf", [128, 4096], BF16)
    hqbf = alloc("hqbf", [128, 4096], BF16)
    hqi = [alloc("hqi0", [128, 512], I32)]
    hqi.append(hqi[0])
    gbuf = alloc("gbuf", [128, 512], F32)
    dbuf = alloc("dbuf", [128, 1024], F32)
    obuf_s = alloc("obuf", [128, 1024], F32)
    obuf = [obuf_s, obuf_s]
    obuf2_s = alloc("obuf2", [128, 1024], F32)
    obuf2 = [obuf2_s, obuf2_s]
    xrel_s = alloc("xrel", [128, 1024], F32)
    xrel = [xrel_s, xrel_s]
    attb_s = alloc("attb", [128, 1024], BF16)
    attb = [attb_s, attb_s]
    C_END = cursor[0]
    assert max(AB_END, C_END) <= 229376, (PERS_END, AB_END, C_END)

    w_units = []
    w_pumped = [0]

    def pump_w(k, cap=None):
        lim = len(w_units) if cap is None else min(cap, len(w_units))
        while w_pumped[0] < lim and k > 0:
            w_units[w_pumped[0]]()
            w_pumped[0] += 1
            k -= 1

    wb_free = [None, None, None, None]
    wt_free = [None, None]
    w_ucount = [0]
    w_tcount = [0]

    def mk_passA(src_d_, col, kp, j):
        def unit():
            b = w_ucount[0] % 4
            w_ucount[0] += 1
            wws = [wb_free[b]] if wb_free[b] else []
            emit("sync", lambda e: e.dma_start(
                wbuf4[b][:], src_d_[128 * kp:128 * (kp + 1), 1024 * j:1024 * (j + 1)]),
                sem="SW%d" % b, n=16, waits=wws)
            emit("vector", lambda e: e.tensor_reduce(
                wsum[:, col:col + 1], wbuf4[b][:], AX, ALU.add,
                apply_absolute_value=True), sem="V", waits=["SW%d" % b])
            wb_free[b] = ("V", p.val("V"))
        return unit

    def mk_crep(ci, m):
        def unit():
            emit("vector", lambda e: e.tensor_reduce(
                acc2[:], wsum[:, 32 * m:32 * (m + 1)], AX, ALU.add), sem="V", waits=["V", "T"])
            emit("tensor", lambda e: e.matmul(
                ppart[0:1, ci:ci + 1], acc2[:], onesf[:], start=True, stop=True),
                sem="T", waits=["V"])
            emit("vector", lambda e: e.tensor_scalar(
                stage[0:1, 0:1], ppart[0:1, ci:ci + 1], 1.0 / (1024.0 * 4096.0), 1e-5,
                ALU.mult, ALU.max), sem="V", waits=["T", "V"])
            emit("vector", lambda e: e.tensor_scalar(stage[0:1, 2:3], stage[0:1, 0:1], 0.5, None, ALU.mult),
                 sem="V", waits=["V"])
            emit("vector", lambda e: e.reciprocal(stage[0:1, 3:4], stage[0:1, 0:1]), sem="V", waits=["V"])
            emit("tensor", lambda e: e.matmul(
                pcrep[:, ci:ci + 2], onesrow_f[:], stage[0:1, 2:4], start=True, stop=True),
                sem="T", waits=["V"])
            emit("vector", lambda e: e.tensor_copy(crep[:, ci:ci + 2], pcrep[:, ci:ci + 2]),
                 sem="V", waits=["T"])
        return unit

    def mk_passB(src_d_, dst_d_, ci, kp, j):
        def unit():
            b = w_ucount[0] % 4
            w_ucount[0] += 1
            tb_ = w_tcount[0] % 2
            w_tcount[0] += 1
            wws = [wb_free[b]] if wb_free[b] else []
            emit("sync", lambda e: e.dma_start(
                wbuf4[b][:], src_d_[128 * kp:128 * (kp + 1), 1024 * j:1024 * (j + 1)]),
                sem="SW%d" % b, n=16, waits=wws)
            emit("scalar", lambda e: e.activation(
                ws1[:], wbuf4[b][:], AF.Sign, bias=consts[:, 0:1], scale=crep[:, ci + 1:ci + 2]),
                waits=["SW%d" % b, "V"])
            emit("scalar", lambda e: e.activation(
                ws2[:], wbuf4[b][:], AF.Sign, bias=consts[:, 1:2], scale=crep[:, ci + 1:ci + 2]),
                sem="SC")
            wb_free[b] = ("SC", p.val("SC"))
            wws2 = ["SC"] + ([wt_free[tb_]] if wt_free[tb_] else [])
            emit("vector", lambda e: e.tensor_add(wternb[tb_][:], ws1[:], ws2[:]),
                 sem="V", waits=wws2)
            emit("gpsimd", lambda e: e.dma_start(
                dst_d_[128 * kp:128 * (kp + 1), 1024 * j:1024 * (j + 1)], wternb[tb_][:]),
                sem="GW%d" % tb_, n=16, waits=["V"])
            wt_free[tb_] = ("GW%d" % tb_, p.val("GW%d" % tb_))
        return unit

    if "W" in phases:
        for mi, (src_, dst_, ci_) in enumerate([(wgT_d, terng_d, 0), (wuT_d, ternu_d, 2)]):
            for kp in range(8):
                for j in range(4):
                    w_units.append(mk_passA(src_, 32 * mi + kp * 4 + j, kp, j))
        w_units.append(mk_crep(0, 0))
        w_units.append(mk_crep(2, 1))
        for kp in range(8):
            for j in range(4):
                w_units.append(mk_passB(wgT_d, terng_d, 0, kp, j))
        for kp in range(32):
            w_units.append(mk_passA(wdT_d, 64 + kp, kp, 0))
        w_units.append(mk_crep(4, 2))
        for kp in range(8):
            for j in range(4):
                w_units.append(mk_passB(wuT_d, ternu_d, 2, kp, j))
        for kp in range(32):
            w_units.append(mk_passB(wdT_d, ternd_d, 4, kp, 0))

    for b in (
        lambda e: e.memset(consts[:, 0:1], -0.5),
        lambda e: e.memset(consts[:, 1:2], 0.5),
        lambda e: e.memset(consts[:, 2:3], -20.0),
        lambda e: e.memset(consts[0:32, 3:4], -1.0),
        lambda e: e.memset(consts[32:64, 3:4], 1.0),
        lambda e: e.memset(consts[64:96, 3:4], -1.0),
        lambda e: e.memset(consts[96:128, 3:4], 1.0),
        lambda e: e.memset(onesf[:], 1.0),
        lambda e: e.memset(ones_bf[:], 1.0),
        lambda e: e.memset(onesrow_f[:], 1.0),
        lambda e: e.memset(onesrow_bf[:], 1.0),
        lambda e: e.memset(ident[:], 0.0),
        lambda e: e.affine_select(out=ident[:], in_=ident[:], compare_op=ALU.not_equal,
                                  fill=1.0, base=0, pattern=[[-1, 128]], channel_multiplier=1),
        lambda e: e.memset(maskAB[:], 0.0),
        lambda e: e.affine_select(out=maskAB[:, 0:256], in_=maskAB[:, 0:256], compare_op=ALU.is_ge,
                                  fill=-1e9, base=0, pattern=[[1, 256]], channel_multiplier=-1),
    ):
        emit("gpsimd", b)
    emit("gpsimd", lambda e: e.affine_select(out=maskAB[:, 256:512], in_=maskAB[:, 256:512],
                                             compare_op=ALU.is_ge, fill=-1e9, base=-128,
                                             pattern=[[1, 256]], channel_multiplier=-1), sem="GP")

    for off in (0, 32, 64, 96):
        emit("sync", (lambda off: lambda e: e.dma_start(fbuf[off:off + 32, :], invf_hi_d[:]))(off),
             sem="dS", n=16)
        emit("sync", (lambda off: lambda e: e.dma_start(flbuf[off:off + 32, :], invf_lo_d[:]))(off),
             sem="dS", n=16)
    emit("sync", lambda e: e.dma_start(bpf[0:1, :], bproj_d[:]), sem="dS", n=16)
    emit("vector", lambda e: e.tensor_copy(bproj_bf[0:1, :], bpf[0:1, :]), waits=["dS"])
    def emit_wrap(dst_fn, src_fn, n, ta, tb, sem=None):
        emit("vector", lambda e: e.tensor_scalar(ta[:, 0:n], src_fn(), -PI, None, ALU.is_lt),
             sem="V", waits=["V"])
        emit("vector", lambda e: e.tensor_scalar(tb[:, 0:n], src_fn(), PI, None, ALU.is_gt),
             sem="V", waits=["V"])
        emit("vector", lambda e: e.tensor_sub(ta[:, 0:n], ta[:, 0:n], tb[:, 0:n]),
             sem="V", waits=["V"])
        emit("vector", lambda e: e.scalar_tensor_tensor(
            dst_fn(), ta[:, 0:n], TWO_PI, src_fn(), ALU.mult, ALU.add), sem=(sem or "V"), waits=["V"])

    emit("vector", lambda e: e.memset(utab[:, 0:1], 0.0), sem="V")
    emit("vector", lambda e: e.tensor_add(utab[:, 1:2], fbuf[:], flbuf[:]), sem="V", waits=["V"])
    sz = 1
    while sz < S:
        if sz >= 2:
            hs_ = sz // 2
            emit("vector", (lambda hs_: lambda e: e.tensor_scalar(
                utmp[:, 0:1], utab[:, hs_:hs_ + 1], utab[:, hs_:hs_ + 1], None, ALU.add))(hs_),
                sem="V", waits=["V"])
            emit_wrap((lambda sz: lambda: utab[:, sz:sz + 1])(sz), lambda: utmp[:, 0:1],
                      1, ts1, ts2)
        n = min(sz, S - sz)
        emit("vector", (lambda sz, n: lambda e: e.tensor_scalar(
            utmp[:, 0:n], utab[:, 0:n], utab[:, sz:sz + 1], None, ALU.add))(sz, n),
            sem="V", waits=["V"])
        emit_wrap((lambda sz, n: lambda: utab[:, sz:sz + n])(sz, n),
                  (lambda n: lambda: utmp[:, 0:n])(n), n, ts1, ts2, sem="V")
        pump_w(1, cap=64)
        sz *= 2
    emit("scalar", lambda e: e.activation(ts1[:, 0:S], utab[:], AF.Sin, bias=0.0, scale=1.0),
         sem="SC", waits=["V"])
    emit("vector", lambda e: e.tensor_scalar(sinS[:], ts1[:, 0:S], consts[:, 3:4], None, ALU.mult),
         waits=["SC", "GP"])
    emit("vector", lambda e: e.tensor_scalar(ts2[:, 0:S], utab[:], HALF_PI, None, ALU.add), sem="V")
    emit_wrap(lambda: utmp2[:, 0:S], lambda: ts2[:, 0:S], S, ts1, utab, sem="V")
    emit("scalar", lambda e: e.activation(cosF[:], utmp2[:, 0:S], AF.Sin, bias=0.0, scale=1.0),
         sem="SC", waits=["V"])
    pump_w(3, cap=64)

    t_tbl_sc = p.val("SC")

    if True:
      with (
        nc.psum_tensor("pqkv0", [128, 512], F32) as pq0,
        nc.psum_tensor("pqkv1", [128, 512], F32) as pq1,
        nc.psum_tensor("ptr0", [128, 1024], BF16) as pt0,
        nc.psum_tensor("ptr1", [128, 1024], BF16) as pt1,
        nc.psum_tensor("ppart", [1, 8], F32) as ppart,
        nc.psum_tensor("pcrep", [128, 8], F32) as pcrep,
      ):
        pqkv = [pq0, pq1]
        ptr = [pt0, pt1]
        xb_free = [[], []]
        ptr_free = [("V", 0), ("V", 0)]
        xqs_free = [("GA0", 0), ("GA1", 0)]
        xnfm_r = xnfm[:].rearrange("p (c tb w) -> p c tb w", c=NCH, tb=NT)
        xq_dst = xqfm_d[:].rearrange("(c p) (tb w) -> p c tb w", p=128, w=128)
        A1R = range(A1N) if ("A" in phases or "1" in phases) else []

        def dma_x(t, par):
            emit("sync", (lambda t, par: lambda e: e.dma_start(
                xbuf[par][:], x_d[128 * t:128 * (t + 1), :]))(t, par),
                sem="SA%d" % par, n=16, waits=list(xb_free[par]))
        for t in A1R:
            par = t % 2
            dma_x(t, par)
            emit("vector", (lambda t, par: lambda e: e.reduce_sum(
                ast[:, t:t + 1], xbuf[par][:], AX))(t, par), sem="V", waits=["SA%d" % par])
            emit("scalar", (lambda t, par: lambda e: e.activation(
                xsq[:], xbuf[par][:], AF.Square, bias=0.0, scale=1.0,
                accum_out=ast[:, 16 + t:17 + t]))(t, par), sem="SC", waits=["SA%d" % par])
            xb_free[par] = [("V", p.val("V")), ("SC", p.val("SC"))]
            pump_w(2)
        if A1N:
            emit("vector", lambda e: e.tensor_scalar(ast[:, 32:48], ast[:, 0:16], 1.0 / D, None, ALU.mult),
                 sem="V", waits=["V", "SC"])
            emit("vector", lambda e: e.tensor_scalar(ast[:, 48:64], ast[:, 16:32], 1.0 / D, 1e-5, ALU.mult, ALU.add),
                 sem="V", waits=["V"])
            emit("vector", lambda e: e.tensor_mul(ast[:, 64:80], ast[:, 32:48], ast[:, 32:48]),
                 sem="V", waits=["V"])
            emit("vector", lambda e: e.tensor_sub(ast[:, 80:96], ast[:, 48:64], ast[:, 64:80]),
                 sem="V", waits=["V"])
            emit("scalar", lambda e: e.sqrt(ast[:, 96:112], ast[:, 80:96]), sem="SC", waits=["V"])
            emit("vector", lambda e: e.reciprocal(ast[:, 112:128], ast[:, 96:112]), sem="V", waits=["SC", "V"])
            emit("vector", lambda e: e.tensor_mul(ast[:, 128:144], ast[:, 32:48], ast[:, 112:128]),
                 sem="V", waits=["V"])
            pump_w(2)
        for t in A1R:
            par = t % 2
            dma_x(t, par)
            emit("vector", (lambda t, par: lambda e: e.tensor_scalar(
                xsq[:], xbuf[par][:], ast[:, 112 + t:113 + t], ast[:, 128 + t:129 + t],
                ALU.mult, ALU.subtract))(t, par), sem="V", waits=["SA%d" % par])
            emit("vector", (lambda t: lambda e: e.tensor_reduce(
                ast[:, 144 + t:145 + t], xsq[:], AX, ALU.max,
                apply_absolute_value=True))(t), sem="V", waits=["V"])
            xb_free[par] = [("V", p.val("V"))]
            pump_w(3)
        if A1N:
            emit("vector", lambda e: e.tensor_scalar(ast[:, 160:176], ast[:, 144:160], 1e-5, None, ALU.max),
                 sem="V", waits=["V"])
            emit("vector", lambda e: e.tensor_scalar(cx[:, 0:NT], ast[:, 160:176], 1.0 / 127.0, None, ALU.mult),
                 sem="V", waits=["V"])
            emit("vector", lambda e: e.reciprocal(ast[:, 176:192], ast[:, 160:176]), sem="V", waits=["V"])
            emit("vector", lambda e: e.tensor_scalar(ast[:, 192:208], ast[:, 176:192], 127.0, None, ALU.mult),
                 sem="V", waits=["V"])
            emit("vector", lambda e: e.tensor_mul(ast[:, 208:224], ast[:, 112:128], ast[:, 192:208]),
                 sem="V", waits=["V"])
            emit("vector", lambda e: e.tensor_mul(ast[:, 224:240], ast[:, 128:144], ast[:, 192:208]),
                 sem="V", waits=["V"])
            pump_w(2)
        for t in A1R:
            par = t % 2
            dma_x(t, par)
            emit("vector", (lambda t, par: lambda e: e.tensor_scalar(
                xnb[par][:], xbuf[par][:], ast[:, 112 + t:113 + t], ast[:, 128 + t:129 + t],
                ALU.mult, ALU.subtract))(t, par), sem="V", waits=["SA%d" % par])
            for c in range(NCH):
                emit("tensor", (lambda c, par: lambda e: e.transpose(
                    ptr[par][:, 128 * c:128 * (c + 1)], xnb[par][:, 128 * c:128 * (c + 1)], ident[:]))(c, par),
                    waits=(["V", ptr_free[par]] if c == 0 else []),
                    sem=("T" if c == NCH - 1 else None))
            emit("vector", (lambda t, par: lambda e: e.tensor_scalar(
                xqi[:], xbuf[par][:], ast[:, 208 + t:209 + t], ast[:, 224 + t:225 + t],
                ALU.mult, ALU.subtract))(t, par))
            emit("vector", (lambda par: lambda e: e.tensor_scalar(
                xqb[par][:], xqi[:], 1.0, None, ALU.mult))(par), sem="V")
            xb_free[par] = [("V", p.val("V"))]
            emit("vector", (lambda t, par: lambda e: e.tensor_copy(
                xnfm_r[:, 0:4, t, :], ptr[par][:, 0:512].rearrange("p (c w) -> p c w", c=4)))(t, par),
                waits=["T"])
            emit("vector", (lambda t, par: lambda e: e.tensor_copy(
                xnfm_r[:, 4:8, t, :], ptr[par][:, 512:1024].rearrange("p (c w) -> p c w", c=4)))(t, par),
                sem="V")
            for c in range(NCH):
                emit("tensor", (lambda c, par: lambda e: e.transpose(
                    ptr[par][:, 128 * c:128 * (c + 1)], xqb[par][:, 128 * c:128 * (c + 1)], ident[:]))(c, par),
                    waits=(["V"] if c == 0 else []),
                    sem=("T" if c == NCH - 1 else None))
            if "xq" not in A1SKIP:
                emit("scalar", (lambda par: lambda e: e.activation(
                    xqstg[par][:], ptr[par][:], AF.Copy, bias=0.0, scale=1.0))(par),
                    sem="SC", waits=["T", xqs_free[par]])
                ptr_free[par] = ("SC", p.val("SC"))
                emit("gpsimd", (lambda t, par: lambda e: e.dma_start(
                    xq_dst[:, :, t, :], xqstg[par][:].rearrange("p (c w) -> p c w", c=NCH)))(t, par),
                    sem="GA%d" % par, n=16, waits=["SC"])
                xqs_free[par] = ("GA%d" % par, p.val("GA%d" % par))
            else:
                ptr_free[par] = ("T", p.val("T"))
            pump_w(3)

        a1_dg = p.val("dG")
        wq_free = [0, 0]
        pq_free = [0, 0]
        qs_free = [0, 0]
        vt_free = [0, 0]
        ptr_free = [("SC", p.val("SC")), ("SC", p.val("SC"))]
        rope_pend = []

        def flush_rope(all_=True):
            while rope_pend if all_ else (len(rope_pend) > 1):
                (qpar, tb2, dest2, cch2, gaval) = rope_pend.pop(0)
                emit("vector", (lambda qpar, tb2: lambda e: e.tensor_mul(
                    tP[:], qsb[qpar][:], cosF[:, 512 * tb2:512 * (tb2 + 1)]))(qpar, tb2),
                    waits=[("RT%d" % qpar, gaval), ("SC", t_tbl_sc)])
                emit("vector", (lambda qpar, tb2: lambda e: e.tensor_mul(
                    tR[:], rotb[qpar][:], sinS[:, 512 * tb2:512 * (tb2 + 1)]))(qpar, tb2))
                emit("vector", (lambda dest2, cch2, tb2: lambda e: e.tensor_add(
                    dest2[:, S * cch2 + 512 * tb2: S * cch2 + 512 * (tb2 + 1)], tP[:], tR[:]))(dest2, cch2, tb2),
                    sem="V")
        vdst = vtm[:].rearrange("p (tk col) -> p tk col", col=D)
        w_flush_waits = []
        for dc in (range(24) if ("A" in phases or "2" in phases) else []):
            if dc == 16:
                pump_w(1 << 30)
                w_flush_waits = [("SC", p.val("SC")), ("GW0", p.val("GW0")),
                                 ("GW1", p.val("GW1"))]
            else:
                pump_w(4)
            par = dc % 2
            emit("sync", (lambda dc, par: lambda e: e.dma_start(
                wqb[par][:], wqkv_d[128 * dc:128 * (dc + 1), :]))(dc, par),
                sem="SA%d" % par, n=16, waits=[("T", wq_free[par])])
            emit("vector", (lambda par: lambda e: e.tensor_copy(wqbf[par][:], wqb[par][:]))(par),
                 sem="V", waits=["SA%d" % par])
            cast_done = p.val("V")
            for tb in range(4):
                ppar = (dc * 4 + tb) % 2
                for di in range(NCH):
                    emit("tensor", (lambda par, ppar, di, tb: lambda e: e.matmul(
                        pqkv[ppar][:], wqbf[par][:, 128 * di:128 * (di + 1)],
                        xnfm[:, S * di + 512 * tb: S * di + 512 * (tb + 1)],
                        start=(di == 0), stop=(di == NCH - 1)))(par, ppar, di, tb),
                        waits=([("V", cast_done), ("V", pq_free[ppar])] if di == 0 else []),
                        sem=("T" if di == NCH - 1 else None))
                if dc < 16:
                    qpar = (dc * 4 + tb) % 4
                    dest = qfm if dc < 8 else kfm
                    cch = dc % 8
                    emit("vector", (lambda ppar, qpar: lambda e: e.tensor_copy(
                        qsb[qpar][:], pqkv[ppar][:]))(ppar, qpar),
                        sem="V", waits=["T", "RT%d" % qpar])
                    pq_free[ppar] = p.val("V")
                    for (dstlo, srclo) in ((0, 32), (32, 0), (64, 96), (96, 64)):
                        emit("gpsimd", (lambda qpar, dstlo, srclo: lambda e: e.dma_start(
                            rotb[qpar][dstlo:dstlo + 32, :], qsb[qpar][srclo:srclo + 32, :]))(qpar, dstlo, srclo),
                            sem="RT%d" % qpar, n=16, waits=["V"])
                    rope_pend.append((qpar, tb, dest, cch, p.val("RT%d" % qpar)))
                    flush_rope(all_=False)
                else:
                    vch = dc - 16
                    vpar = (dc * 4 + tb) % 2
                    emit("vector", (lambda ppar, vpar: lambda e: e.tensor_copy(
                        vtmp[vpar][:], pqkv[ppar][:]))(ppar, vpar), sem="V",
                        waits=["T", ("T", vt_free[vpar]), "GA0", "GA1"])
                    pq_free[ppar] = p.val("V")
                    for s_ in range(4):
                        emit("tensor", (lambda vpar, s_: lambda e: e.transpose(
                            ptr[vpar][:, 128 * s_:128 * (s_ + 1)], vtmp[vpar][:, 128 * s_:128 * (s_ + 1)],
                            ident[:]))(vpar, s_),
                            waits=(["V", ptr_free[vpar]] if s_ == 0 else []),
                            sem=("T" if s_ == 3 else None))
                    vt_free[vpar] = p.val("T")
                    emit("vector", (lambda vpar, vch, tb: lambda e: e.tensor_copy(
                        vdst[:, 4 * tb:4 * (tb + 1), 128 * vch:128 * (vch + 1)],
                        ptr[vpar][:, 0:512].rearrange("p (s w) -> p s w", s=4)))(vpar, vch, tb),
                        sem="V", waits=["T"] + w_flush_waits)
                    w_flush_waits = []
                    ptr_free[vpar] = ("V", p.val("V"))
            wq_free[par] = p.val("T")

        if DBGA2:
            emit("gpsimd", lambda e: e.dma_start(dbgq_d[:], qfm[:]), sem="dG", n=16,
                 waits=["V", ("T", p.val("T"))])
            emit("gpsimd", lambda e: e.dma_start(dbgk_d[:], kfm[:]), sem="dG", n=16)
            emit("gpsimd", lambda e: e.dma_start(dbgv_d[:], vtm[:]), sem="dG", n=16)
        flush_rope()
        wp_t = p.val("T")
        for c in (range(NCH) if ("A" in phases or "2" in phases) else []):
            emit("sync", (lambda c, wp_t: lambda e: e.dma_start(
                wqb[0][:], wproj_d[128 * c:128 * (c + 1), :]))(c, wp_t), sem="SA0", n=16,
                waits=[("T", wp_t), ("V", p.val("V"))])
            emit("vector", (lambda c: lambda e: e.tensor_copy(
                wprojb[:, D * c:D * (c + 1)], wqb[0][:]))(c), sem="V", waits=["SA0"])
        pump_w(1 << 30)

    if "B" in phases:
      with (
        nc.psum_tensor("pSh0", [128, 1024], F32) as pSh0,
        nc.psum_tensor("pSh1", [128, 1024], F32) as pSh1,
        nc.psum_tensor("pO", [128, 256], F32) as pO,
        nc.psum_tensor("pl", [128, 256], F32) as pl,
        nc.psum_tensor("pproj0", [128, 512], F32) as pp0,
        nc.psum_tensor("pproj1", [128, 512], F32) as pp1,
      ):
        pproj = [pp0, pp1]
        att_free = [0, 0]
        for b in range(NB):
            ntk = 2 * (b + 1)
            for cch in range(NCH):
                sc_pair_start = p.val("SC")
                v_norm_done = p.val("V")
                sc_bank_done = [sc_pair_start, sc_pair_start]
                nsteps = ntk // 2

                def emit_qk(jj):
                    bank = jj % 2
                    for j in (2 * jj, 2 * jj + 1):
                        qtr = j % 4
                        for (hs, psum, tpos) in ((0, pSh0, (0, 0)), (64, pSh1, (64, 0))):
                            w = [("SC", sc_bank_done[bank])] if j == 2 * jj and hs == 0 else []
                            emit("tensor", (lambda j, qtr, hs, psum, tpos, cch, b: lambda e: e.matmul(
                                psum[:, 256 * qtr:256 * (qtr + 1)],
                                kfm[hs:hs + 64, S * cch + 128 * j: S * cch + 128 * (j + 1)],
                                qfm[hs:hs + 64, S * cch + 256 * b: S * cch + 256 * (b + 1)],
                                start=True, stop=True,
                                tile_position=tpos))(j, qtr, hs, psum, tpos, cch, b),
                                waits=w, sem=("T" if hs == 64 else None))

                def emit_exp(jj):
                    j1 = 2 * jj + 1
                    qtr0 = (2 * jj) % 4
                    bank = jj % 2
                    tw = [("T", qk_done[jj]), ("T", avl_done.get(jj - 2, 0))]
                    if j1 == ntk - 1:
                        for psum in (pSh0, pSh1):
                            emit("vector", (lambda psum, qtr0: lambda e: e.tensor_tensor(
                                psum[:, 256 * qtr0:256 * qtr0 + 512],
                                psum[:, 256 * qtr0:256 * qtr0 + 512], maskAB[:], ALU.add))(psum, qtr0),
                                sem="V", waits=tw)
                    for (psum, ebuf) in ((pSh0, Eb[0]), (pSh1, Eb[1])):
                        emit("scalar", (lambda psum, ebuf, qtr0, bank: lambda e: e.activation(
                            ebuf[:, 512 * bank:512 * (bank + 1)], psum[:, 256 * qtr0:256 * qtr0 + 512],
                            AF.Exp, bias=consts[:, 2:3], scale=0.125))(psum, ebuf, qtr0, bank),
                            sem="SC", waits=tw + ["V"])
                    sc_bank_done[bank] = p.val("SC")
                    exp_done[jj] = p.val("SC")

                def emit_avl(jj):
                    bank = jj % 2
                    for j in (2 * jj, 2 * jj + 1):
                        qtr = j % 4
                        eoff = 256 * (qtr % 2) + 512 * bank
                        first = (j == 0)
                        last = (j == ntk - 1)
                        for (hs, ebuf, tpos) in ((0, Eb[0], (0, 0)), (64, Eb[1], (0, 64))):
                            w = []
                            if hs == 0:
                                w = [("SC", exp_done[jj])]
                                if first:
                                    w.append(("V", v_norm_done))
                            emit("tensor", (lambda j, hs, ebuf, eoff, tpos, first, last, cch: lambda e: e.matmul(
                                pO[hs:hs + 64, :],
                                vtm[:, D * j + 64 * (2 * cch + hs // 64): D * j + 64 * (2 * cch + hs // 64) + 64],
                                ebuf[:, eoff:eoff + 256],
                                start=first, stop=last,
                                tile_position=tpos))(j, hs, ebuf, eoff, tpos, first, last, cch),
                                waits=w)
                        for (hs, ebuf, tpos) in ((0, Eb[0], (0, 0)), (64, Eb[1], (0, 64))):
                            emit("tensor", (lambda hs, ebuf, eoff, tpos, first, last: lambda e: e.matmul(
                                pl[hs:hs + 64, :], ones_bf[:], ebuf[:, eoff:eoff + 256],
                                start=first, stop=last,
                                tile_position=tpos))(hs, ebuf, eoff, tpos, first, last),
                                sem=("T" if hs == 64 else None))
                    avl_done[jj] = p.val("T")

                qk_done = {}
                exp_done = {}
                avl_done = {}
                for jj in range(nsteps):
                    emit_qk(jj)
                    qk_done[jj] = p.val("T")
                    if jj >= 1:
                        emit_exp(jj - 1)
                    if jj >= 2:
                        emit_avl(jj - 2)
                emit_exp(nsteps - 1)
                if nsteps >= 2:
                    emit_avl(nsteps - 2)
                emit_avl(nsteps - 1)
                emit("scalar", lambda e: e.activation(lnl[:], pl[:], AF.Ln, bias=0.0, scale=1.0),
                     sem="SC", waits=["T"])
                emit("scalar", lambda e: e.activation(recl[:], lnl[:], AF.Exp, bias=0.0, scale=-1.0),
                     sem="SC", waits=["SC"])
                emit("vector", (lambda cch: lambda e: e.tensor_tensor(
                    Ofm[:, 256 * cch:256 * (cch + 1)], pO[:], recl[:], ALU.mult))(cch),
                    sem="V", waits=["SC"])
            if DBGA2:
                emit("gpsimd", (lambda b: lambda e: e.dma_start(
                    dbgo_d[128 * b:128 * (b + 1), :], Ofm[:]))(b), sem="dG", n=16, waits=["V"])
            for m2 in range(2):
                for nn in range(2):
                    for c2 in range(NCH):
                        emit("tensor", (lambda m2, nn, c2: lambda e: e.matmul(
                            pproj[nn][:], Ofm[:, 256 * c2 + 128 * m2: 256 * c2 + 128 * (m2 + 1)],
                            wprojb[:, D * c2 + 512 * nn: D * c2 + 512 * (nn + 1)],
                            start=(c2 == 0), stop=False))(m2, nn, c2),
                            waits=(["V"] if c2 == 0 else []))
                    emit("tensor", (lambda nn: lambda e: e.matmul(
                        pproj[nn][:], onesrow_bf[:], bproj_bf[0:1, 512 * nn:512 * (nn + 1)],
                        start=False, stop=True))(nn), sem="T")
                    emit("vector", (lambda nn: lambda e: e.tensor_copy(
                        attst[nn][:], pproj[nn][:]))(nn), sem="V",
                        waits=["T", "GA%d" % nn])
                    emit("gpsimd", (lambda b, m2, nn: lambda e: e.dma_start(
                        attn_d[256 * b + 128 * m2: 256 * b + 128 * (m2 + 1), 512 * nn:512 * (nn + 1)],
                        attst[nn][:]))(b, m2, nn), sem="GA%d" % nn, n=16, waits=["V"])

    p.barrier()

    if "C" in phases:
      with (
        nc.psum_tensor("pg0", [128, 512], F32) as pg0,
        nc.psum_tensor("pg1", [128, 512], F32) as pg1,
        nc.psum_tensor("pu0", [128, 512], F32) as pu0,
        nc.psum_tensor("pu1", [128, 512], F32) as pu1,
        nc.psum_tensor("pd0", [128, 512], F32) as pd0,
        nc.psum_tensor("pd1", [128, 512], F32) as pd1,
        nc.psum_tensor("ptrh", [128, 1024], BF16) as ptrh,
      ):
        pg = [pg0, pg1]
        pu = [pu0, pu1]
        pd = [pd0, pd1]
        xq_src = xqfm_d[:].rearrange("(c p) s -> p c s", p=128)
        tg_src = terng_d[:].rearrange("(c p) q -> p c q", p=128)
        tu_src = ternu_d[:].rearrange("(c p) q -> p c q", p=128)
        hqfm_w = hqfm[:].rearrange("p (j mm w) -> p j mm w", mm=4, w=128)
        ptrh_r = ptrh[:].rearrange("p (j w) -> p j w", w=128)
        tern_free = [0, 0]
        terndP_free = [0, 0]
        t_gate_done = 0
        x_free = [0, 0]
        o_free = [0, 0]
        hq_done = 0
        for M in range(CN):
            emit("sync", (lambda M: lambda e: e.dma_start(
                xqf[:].rearrange("p (c w) -> p c w", c=NCH),
                xq_src[:, :, 512 * M:512 * (M + 1)]))(M),
                sem="XQ", n=16, waits=[("T", (t_gate_done if M > 0 else p.val("T")))])
            for m in range(4):
                emit("vector", (lambda M, m: lambda e: e.tensor_scalar(
                    hstats[:, 40 + m:41 + m], cx[:, 4 * M + m:4 * M + m + 1],
                    crep[:, 0:1], None, ALU.mult))(M, m), sem="V", waits=["V"])
                emit("vector", (lambda M, m: lambda e: e.tensor_scalar(
                    hstats[:, 44 + m:45 + m], cx[:, 4 * M + m:4 * M + m + 1],
                    crep[:, 2:3], None, ALU.mult))(M, m), sem="V", waits=["V"])
            gu_done = [p.val("V"), p.val("V")]
            for ic in (range(8) if "gu" not in CSKIP else []):
                tpar = ic % 2
                emit("sync", (lambda ic, tpar: lambda e: e.dma_start(
                    terng[tpar][:].rearrange("p (c w) -> p c w", c=NCH),
                    tg_src[:, :, 512 * ic:512 * (ic + 1)]))(ic, tpar),
                    sem="TG%d" % tpar, n=16, waits=[("V", tern_free[tpar])])
                emit("sync", (lambda ic, tpar: lambda e: e.dma_start(
                    ternu[tpar][:].rearrange("p (c w) -> p c w", c=NCH),
                    tu_src[:, :, 512 * ic:512 * (ic + 1)]))(ic, tpar),
                    sem="TG%d" % tpar, n=16)
                for m in range(4):
                    gpar = (ic * 4 + m) % 2
                    for c in range(NCH):
                        emit("tensor", (lambda c, m, gpar, tpar: lambda e: e.matmul(
                            pg[gpar][:], xqf[:, 512 * c + 128 * m: 512 * c + 128 * (m + 1)],
                            terng[tpar][:, 512 * c:512 * (c + 1)],
                            start=(c == 0), stop=(c == NCH - 1)))(c, m, gpar, tpar),
                            waits=(["TG%d" % tpar, "XQ", ("V", gu_done[gpar])] if c == 0 else []))
                    for c in range(NCH):
                        emit("tensor", (lambda c, m, gpar, tpar: lambda e: e.matmul(
                            pu[gpar][:], xqf[:, 512 * c + 128 * m: 512 * c + 128 * (m + 1)],
                            ternu[tpar][:, 512 * c:512 * (c + 1)],
                            start=(c == 0), stop=(c == NCH - 1)))(c, m, gpar, tpar),
                            sem=("T" if c == NCH - 1 else None))
                    emit("scalar", (lambda m, gpar: lambda e: e.activation(
                        gbuf[:], pg[gpar][:], AF.Silu, bias=0.0,
                        scale=hstats[:, 40 + m:41 + m]))(m, gpar), sem="SC", waits=["T", "V"])
                    emit("vector", (lambda m, ic, gpar: lambda e: e.tensor_tensor(
                        hB[:, 4096 * m + 512 * ic: 4096 * m + 512 * (ic + 1)], gbuf[:],
                        pu[gpar][:], ALU.mult))(m, ic, gpar), waits=["SC"])
                    emit("vector", (lambda m, ic: lambda e: e.tensor_reduce(
                        hstats[:, 8 * m + ic: 8 * m + ic + 1],
                        hB[:, 4096 * m + 512 * ic: 4096 * m + 512 * (ic + 1)], AX, ALU.max,
                        apply_absolute_value=True))(m, ic), sem="V")
                    gu_done[gpar] = p.val("V")
                tern_free[tpar] = p.val("V")
            t_gate_done = p.val("T")

            def emit_tail(m):
                nonlocal hq_done
                if "tail" in CSKIP:
                    return
                emit("vector", (lambda m: lambda e: e.tensor_reduce(
                    hstats[:, 32 + m:33 + m], hstats[:, 8 * m:8 * m + 8], AX, ALU.max))(m),
                    sem="V", waits=["V"])
                emit("vector", (lambda m: lambda e: e.tensor_tensor(
                    hstats[:, 36:37], hstats[:, 32 + m:33 + m], hstats[:, 44 + m:45 + m],
                    ALU.mult))(m), sem="V", waits=["V"])
                emit("vector", lambda e: e.tensor_scalar(hstats[:, 37:38], hstats[:, 36:37], 1e-5, None, ALU.max),
                     sem="V", waits=["V"])
                emit("vector", lambda e: e.reciprocal(hstats[:, 38:39], hstats[:, 37:38]),
                     sem="V", waits=["V"])
                emit("vector", (lambda m: lambda e: e.tensor_tensor(
                    hstats[:, 36:37], hstats[:, 38:39], hstats[:, 44 + m:45 + m], ALU.mult))(m),
                    sem="V", waits=["V"])
                emit("vector", lambda e: e.tensor_scalar(hstats[:, 39:40], hstats[:, 36:37], 127.0, None, ALU.mult),
                     sem="V", waits=["V"])
                emit("vector", (lambda m: lambda e: e.tensor_scalar(
                    hstats[:, 48 + m:49 + m], hstats[:, 37:38], 1.0 / 127.0, None, ALU.mult))(m),
                    sem="V", waits=["V"])
                emit("vector", (lambda m: lambda e: e.tensor_scalar(
                    hstats[:, 48 + m:49 + m], hstats[:, 48 + m:49 + m], crep[:, 4:5], None, ALU.mult))(m),
                    sem="V", waits=["V"])
                for q in range(8):
                    qp = q % 2
                    emit("vector", (lambda m, q, qp: lambda e: e.tensor_scalar(
                        hqi[qp][:], hB[:, 4096 * m + 512 * q: 4096 * m + 512 * (q + 1)],
                        hstats[:, 39:40], None, ALU.mult))(m, q, qp),
                        waits=(["V"] if q == 0 else []))
                    emit("vector", (lambda q, qp: lambda e: e.tensor_scalar(
                        hqbf[:, 512 * q:512 * (q + 1)], hqi[qp][:], 1.0, None, ALU.mult))(q, qp),
                        sem="V")
                for grp in range(4):
                    for i2 in range(8):
                        emit("tensor", (lambda grp, i2: lambda e: e.transpose(
                            ptrh[:, 128 * i2:128 * (i2 + 1)],
                            hqbf[:, 1024 * grp + 128 * i2: 1024 * grp + 128 * (i2 + 1)], ident[:]))(grp, i2),
                            waits=((["V", ("T", hq_done)] if grp == 0 else ["V"]) if i2 == 0 else []),
                            sem=("T" if i2 == 7 else None))
                    emit("vector", (lambda grp, m: lambda e: e.tensor_copy(
                        hqfm_w[:, 8 * grp:8 * (grp + 1), m, :], ptrh_r[:]))(grp, m),
                        sem="V", waits=["T"])
                hq_done = p.val("T")
                if DBGC:
                    emit("gpsimd", (lambda M, m: lambda e: e.dma_start(
                        dbghq_d[512 * M + 128 * m: 512 * M + 128 * (m + 1), :], hqbf[:]))(M, m),
                        sem="dG", n=16, waits=["V"])
                    emit("gpsimd", (lambda M, m: lambda e: e.dma_start(
                        dbghst_d[128 * (4 * M + m): 128 * (4 * M + m + 1), :], hstats[:]))(M, m),
                        sem="dG", n=16)
                    emit("gpsimd", (lambda M, m: lambda e: e.dma_start(
                        dbghqf_d[128 * (4 * M + m): 128 * (4 * M + m + 1), :], hqfm[:]))(M, m),
                        sem="dG", n=16, waits=["V"])
            emit_tail(0)
            emit_tail(1)
            sc_now = p.val("SC")
            tdsrc = ternd_d[:].rearrange("(g p) q -> p g q", p=128)
            for half in ((0, 1) if ("tail" not in CSKIP and "down" not in CSKIP) else ()):
                v_hq_done = p.val("V")
                pdq = [[pd0, pd1], [pg0, pg1]]
                for j2 in range(16):
                    if half == 0 and j2 == 1:
                        emit_tail(2)
                    if half == 0 and j2 == 8:
                        emit_tail(3)
                    jp = j2 % 2
                    emit("sync", (lambda j2, jp: lambda e: e.dma_start(
                        terndP[jp][:].rearrange("p (g q) -> p g q", g=2),
                        tdsrc[:, 2 * j2:2 * j2 + 2, :]))(j2, jp),
                        sem="TD%d" % jp, n=16, waits=[("T", terndP_free[jp])])
                    for jsub in (0, 1):
                        j = 2 * j2 + jsub
                        for m2 in (0, 1):
                            m = 2 * half + m2
                            for nn in range(2):
                                w = []
                                if jsub == 0 and m2 == 0 and nn == 0:
                                    w = ["TD%d" % jp]
                                    if j2 == 0:
                                        w += [("V", v_hq_done), ("SC", sc_now)]
                                emit("tensor", (lambda j, jp, jsub, m, m2, nn: lambda e: e.matmul(
                                    pdq[m2][nn][:], hqfm[:, 512 * j + 128 * m: 512 * j + 128 * (m + 1)],
                                    terndP[jp][:, 1024 * jsub + 512 * nn: 1024 * jsub + 512 * (nn + 1)],
                                    start=(j == 0), stop=(j == 31)))(j, jp, jsub, m, m2, nn),
                                    waits=w,
                                    sem=("T" if (jsub == 1 and m2 == 1 and nn == 1) else None))
                    terndP_free[jp] = p.val("T")
                for m2 in (0, 1):
                    m = 2 * half + m2
                    opar = m % 2
                    for nn in range(2):
                        emit("vector", (lambda m, m2, nn: lambda e: e.tensor_scalar(
                            dbuf[:, 512 * nn:512 * (nn + 1)], pdq[m2][nn][:], hstats[:, 48 + m:49 + m], None,
                            ALU.mult))(m, m2, nn), sem="V", waits=["T", "V"])
                    emit("sync", (lambda M, m, opar: lambda e: e.dma_start(
                        xrel[opar][:], x_d[512 * M + 128 * m: 512 * M + 128 * (m + 1), :]))(M, m, opar),
                        sem="XR", n=16, waits=[("V", x_free[opar])])
                    emit("sync", (lambda M, m, opar: lambda e: e.dma_start(
                        attb[opar][:], attn_d[512 * M + 128 * m: 512 * M + 128 * (m + 1), :]))(M, m, opar),
                        sem="XR", n=16)
                    emit("vector", (lambda opar: lambda e: e.tensor_add(
                        obuf[opar][:], dbuf[:], xrel[opar][:]))(opar),
                        waits=["XR"])
                    emit("vector", (lambda opar: lambda e: e.tensor_add(
                        obuf2[opar][:], obuf[opar][:], attb[opar][:]))(opar), sem="V",
                        waits=["GO"])
                    x_free[0] = p.val("V")
                    x_free[1] = x_free[0]
                    emit("gpsimd", (lambda M, m, opar: lambda e: e.dma_start(
                        out_d[512 * M + 128 * m: 512 * M + 128 * (m + 1), :], obuf2[opar][:]))(M, m, opar),
                        sem="GO", n=16, waits=["V"])
            gu_done_next = p.val("V")
    final_waits = [(s, p.val(s)) for s in SEMNAMES
                   if s not in ("T", "V", "SC", "GP") and p.val(s) > 0]

    SEMS = {}
    sem_guards = []
    for s in SEMNAMES:
        gctx = nc.semaphore(s)
        SEMS[s] = gctx.__enter__()
        sem_guards.append(gctx)
    p.install(nc, SEMS, final_waits)
    for gctx in reversed(sem_guards):
        gctx.__exit__(None, None, None)
    return nc


_NC_CACHE = None


def kernel(L0_attn_x, L0_attn_w_qkv, L0_attn_w_proj, L0_attn_b_proj,
           L0_ffn_w_gate, L0_ffn_w_up, L0_ffn_w_down):
    global _NC_CACHE
    if _NC_CACHE is None:
        _NC_CACHE = build_program()
    nc = _NC_CACHE

    x = np.ascontiguousarray(np.asarray(L0_attn_x, dtype=np.float32))
    wqkv = np.asarray(L0_attn_w_qkv, dtype=np.float32)
    wqkv = np.ascontiguousarray(
        wqkv.reshape(8, 128, 24, 128).transpose(2, 1, 0, 3).reshape(3072, 1024))
    wproj = np.ascontiguousarray(np.asarray(L0_attn_w_proj, dtype=np.float32))
    bproj = np.ascontiguousarray(np.asarray(L0_attn_b_proj, dtype=np.float32)).reshape(1, D)
    wgT = np.ascontiguousarray(np.asarray(L0_ffn_w_gate, dtype=np.float32).T)
    wuT = np.ascontiguousarray(np.asarray(L0_ffn_w_up, dtype=np.float32).T)
    wdT = np.ascontiguousarray(np.asarray(L0_ffn_w_down, dtype=np.float32).T)

    iv64 = 1.0 / (10000.0 ** (np.arange(0, 64, 2, dtype=np.float64) / 64.0))
    invf_hi = iv64.astype(np.float32).reshape(32, 1)
    invf_lo = (iv64 - invf_hi.astype(np.float64).reshape(-1)).astype(np.float32).reshape(32, 1)

    shared = {
        "wqkv": wqkv, "wproj": wproj, "bproj": bproj,
        "wgT": wgT, "wuT": wuT, "wdT": wdT,
        "invf_hi": invf_hi, "invf_lo": invf_lo,
    }
    in_maps = [dict(shared, x=np.ascontiguousarray(x[b])) for b in range(8)]
    res = run_bass_kernel_spmd(nc, in_maps, core_ids=list(range(8)))
    out = np.stack([np.asarray(res.results[b]["out"]) for b in range(8)], axis=0)
    return out.astype(np.float32)


# version 19
# speedup vs baseline: 1.0474x; 1.0474x over previous
import numpy as np
import ml_dtypes
import concourse.bass as bass
import concourse.mybir as mybir
from concourse.bass_utils import run_bass_kernel_spmd

F32 = mybir.dt.float32
BF16 = mybir.dt.bfloat16
I32 = mybir.dt.int32
AF = mybir.ActivationFunctionType
ALU = mybir.AluOpType
AX = mybir.AxisListType.X

S = 2048
D = 1024
INNER = 4096
NT = S // 128
NCH = D // 128
NB = S // 256
PI = float(np.float32(np.pi))
TWO_PI = float(np.float32(2.0 * np.pi))
HALF_PI = float(np.float32(np.pi / 2.0))

ENGINES = ("sync", "gpsimd", "scalar", "vector", "tensor")
SEMNAMES = ("dS", "dG", "SA0", "SA1", "TG0", "TG1", "TD0", "TD1", "XR", "XQ",
            "GA0", "GA1", "GO", "RT0", "RT1", "RT2", "RT3", "T", "V", "SC", "GP",
            "SW0", "SW1", "SW2", "SW3", "GW0", "GW1")
DTSIZE = {F32: 4, BF16: 2, I32: 4}


class Prog:
    def __init__(self):
        self.streams = {e: [] for e in ENGINES}
        self.tot = {s: 0 for s in SEMNAMES}

    def val(self, sem):
        return self.tot[sem]

    def emit(self, eng, build, sem=None, n=1, waits=()):
        wv = []
        for w in waits:
            if isinstance(w, tuple):
                s, v = w
                wv.append((s, self.tot[s] if v is None else v))
            else:
                wv.append((w, self.tot[w]))
        if sem is not None:
            self.tot[sem] += n

        def fn(e, SEMS, waited):
            for s, v in wv:
                if v > 0 and waited.get(s, 0) < v:
                    e.wait_ge(SEMS[s], v)
                    waited[s] = v
            ins = build(e)
            if sem is not None and ins is not None:
                ins.then_inc(SEMS[sem], n)

        self.streams[eng].append(fn)

    def barrier(self):
        snap = dict(self.tot)
        owner = {"dS": "sync", "dG": "gpsimd", "GP": "gpsimd",
                 "SA0": "sync", "SA1": "sync", "TG0": "sync", "TG1": "sync",
                 "TD0": "sync", "TD1": "sync", "XR": "sync", "XQ": "sync",
                 "GA0": "gpsimd", "GA1": "gpsimd", "GO": "gpsimd",
                 "RT0": "gpsimd", "RT1": "gpsimd", "RT2": "gpsimd", "RT3": "gpsimd",
                 "SW0": "sync", "SW1": "sync", "SW2": "sync", "SW3": "sync",
                 "GW0": "gpsimd", "GW1": "gpsimd",
                 "T": "tensor", "V": "vector", "SC": "scalar"}
        for eng in ENGINES:
            waits = [(s, snap[s]) for s in SEMNAMES if owner[s] != eng and snap[s] > 0]
            self.emit(eng, lambda e: None, waits=waits)

    def install(self, nc, SEMS, final_waits):
        with nc.Block() as blk:
            for eng in ENGINES:
                fns = list(self.streams[eng])

                def make(fns, eng):
                    def body(e):
                        waited = {}
                        for fn in fns:
                            fn(e, SEMS, waited)
                        if eng == "sync":
                            for s, v in final_waits:
                                e.wait_ge(SEMS[s], v)
                    return body

                getattr(blk, eng)(make(fns, eng))


def build_program(phases="TWABC"):
    import os
    A1N = int(os.environ.get("A1N", str(NT)))
    A1SKIP = set(os.environ.get("A1SKIP", "").split(","))
    CN = int(os.environ.get("CN", "4"))
    CSKIP = set(os.environ.get("CSKIP", "").split(","))
    nc = bass.Bass("TRN2", target_bir_lowering=False)
    p = Prog()

    def emit(eng, b, **kw):
        p.emit(eng, b, **kw)

    x_d = nc.dram_tensor("x", [S, D], F32, kind="ExternalInput")
    wqkv_d = nc.dram_tensor("wqkv", [3 * D, D], F32, kind="ExternalInput")
    wproj_d = nc.dram_tensor("wproj", [D, D], F32, kind="ExternalInput")
    bproj_d = nc.dram_tensor("bproj", [1, D], F32, kind="ExternalInput")
    wgT_d = nc.dram_tensor("wgT", [D, INNER], F32, kind="ExternalInput")
    wuT_d = nc.dram_tensor("wuT", [D, INNER], F32, kind="ExternalInput")
    wdT_d = nc.dram_tensor("wdT", [INNER, D], F32, kind="ExternalInput")
    invf_hi_d = nc.dram_tensor("invf_hi", [32, 1], F32, kind="ExternalInput")
    invf_lo_d = nc.dram_tensor("invf_lo", [32, 1], F32, kind="ExternalInput")
    out_d = nc.dram_tensor("out", [S, D], F32, kind="ExternalOutput")

    SCRK = "ExternalOutput" if os.environ.get("DBG_SCRATCH") else "Internal"
    DBGA1 = bool(os.environ.get("DBG_A1"))
    if DBGA1:
        dbgst_d = nc.dram_tensor("dbgst", [NT * 128, 16], F32, kind="ExternalOutput")
        dbgxn_d = nc.dram_tensor("dbgxn", [NT * 128, D], F32, kind="ExternalOutput")
    DBGC = bool(os.environ.get("DBG_C"))
    if DBGC:
        dbghq_d = nc.dram_tensor("dbghq", [S, INNER], BF16, kind="ExternalOutput")
        dbghst_d = nc.dram_tensor("dbghst", [NT * 128, 48], F32, kind="ExternalOutput")
        dbghqf_d = nc.dram_tensor("dbghqf", [NT * 128, 16384], BF16, kind="ExternalOutput")
    DBGA2 = bool(os.environ.get("DBG_A2"))
    if DBGA2:
        dbgq_d = nc.dram_tensor("dbgq", [128, NCH * S], F32, kind="ExternalOutput")
        dbgk_d = nc.dram_tensor("dbgk", [128, NCH * S], F32, kind="ExternalOutput")
        dbgv_d = nc.dram_tensor("dbgv", [128, NT * D], F32, kind="ExternalOutput")
        dbgo_d = nc.dram_tensor("dbgo", [NB * 128, 8 * 256], F32, kind="ExternalOutput")
    terng_d = nc.dram_tensor("terng_scr", [D, INNER], BF16, kind=SCRK)
    ternu_d = nc.dram_tensor("ternu_scr", [D, INNER], BF16, kind=SCRK)
    ternd_d = nc.dram_tensor("ternd_scr", [INNER, D], BF16, kind=SCRK)
    xqfm_d = nc.dram_tensor("xqfm_scr", [D, S], BF16, kind=SCRK)
    attn_d = nc.dram_tensor("attn_scr", [S, D], BF16, kind=SCRK)

    BASE = 24576
    cursor = [BASE]

    OFFS = {}

    def alloc(name, shape, dtype, at=None):
        nbytes = int(np.prod(shape[1:])) * DTSIZE[dtype]
        off = cursor[0] if at is None else at
        assert off % 32 == 0, (name, off)
        t = nc.alloc_sbuf_tensor_at(name, list(shape), dtype, offset=off)
        OFFS[id(t)] = off
        if at is None:
            cursor[0] = off + ((nbytes + 31) // 32) * 32
        return t

    def offof(t):
        return OFFS[id(t)]

    consts = alloc("consts", [128, 8], F32)
    onesf = alloc("onesf", [128, 1], F32)
    ones_bf = alloc("ones_bf", [128, 64], BF16)
    onesrow_f = alloc("onesrow_f", [1, 128], F32)
    onesrow_bf = alloc("onesrow_bf", [1, 128], BF16)
    ident = alloc("ident", [128, 128], BF16)
    Prot = alloc("Prot", [128, 128], BF16)
    maskAB = alloc("maskAB", [128, 512], F32)
    cosF = alloc("cosF", [128, S], BF16)
    sinS = alloc("sinS", [128, S], BF16)
    cx = alloc("cx", [128, NT], F32)
    crep = alloc("crep", [128, 8], F32)
    stage = alloc("stage", [1, 8], F32)
    bproj_bf = alloc("bproj_bf", [1, D], BF16)
    stats = alloc("stats", [128, 16], F32)
    hstats = alloc("hstats", [128, 56], F32)
    PERS_END = cursor[0]

    cursor[0] = PERS_END
    xnfm = alloc("xnfm", [128, NCH * S], BF16)
    kfm = alloc("kfm", [128, NCH * S], BF16)
    qfm = alloc("qfm", [128, NCH * S], BF16)
    vtm = alloc("vtm", [128, NT * D], BF16)
    wprojb = alloc("wprojb", [128, NCH * D], BF16, at=PERS_END)
    xbuf = [alloc("xbuf0", [128, D], F32), alloc("xbuf1", [128, D], F32)]
    xntm = alloc("xntm", [128, D], F32)
    xqi = alloc("xqi", [128, D], I32)
    xqb = [alloc("xqb0", [128, D], BF16), alloc("xqb1", [128, D], BF16)]
    xnb = [alloc("xnb0", [128, D], BF16), alloc("xnb1", [128, D], BF16)]
    xqstg = [alloc("xqstg0", [128, D], BF16), alloc("xqstg1", [128, D], BF16)]
    ast = alloc("ast", [128, 240], F32, at=offof(xntm))
    wqb = [alloc("wqb0", [128, 1024], F32), alloc("wqb1", [128, 1024], F32)]
    wqbf = [alloc("wqbf0", [128, 1024], BF16), alloc("wqbf1", [128, 1024], BF16)]
    qsb = [alloc("qsb%d" % i, [128, 512], BF16) for i in range(4)]
    tP = alloc("tP", [128, 512], BF16)
    tR = alloc("tR", [128, 512], BF16)
    AB_END = cursor[0]
    utab = alloc("utab", [128, S], F32, at=offof(kfm))
    utmp2 = alloc("utmp2", [128, S], F32, at=offof(kfm) + 8192)
    ts1 = alloc("ts1", [128, S], F32, at=offof(kfm) + 16384)
    ts2 = alloc("ts2", [128, S], F32, at=offof(kfm) + 24576)
    utmp = alloc("utmp", [128, 1024], F32, at=offof(qfm))
    bpf = alloc("bpf", [1, D], F32, at=offof(qfm) + 4096)
    fbuf = alloc("fbuf", [128, 1], F32, at=offof(qfm) + 8192)
    flbuf = alloc("flbuf", [128, 1], F32, at=offof(qfm) + 8224)
    wbuf4 = [alloc("wb4_%d" % i, [128, 1024], F32, at=offof(vtm) + 4096 * i) for i in range(4)]
    wsA = [alloc("wsA0", [128, 1024], F32, at=offof(vtm) + 16384),
           alloc("wsA1", [128, 1024], F32, at=offof(vtm) + 20480)]
    wsB = [alloc("wsB0", [128, 1024], F32, at=offof(vtm) + 24576),
           alloc("wsB1", [128, 1024], F32, at=offof(vtm) + 28672)]
    wternb = [alloc("wternb0", [128, 1024], BF16), alloc("wternb1", [128, 1024], BF16)]
    wsum = alloc("wsum", [128, 96], F32)
    acc2 = alloc("acc2", [128, 1], F32)
    xsq = alloc("xsq", [128, 1024], F32, at=offof(xqi))
    vtmp = [alloc("vtmp0", [128, 512], BF16, at=offof(xqstg[0])),
            alloc("vtmp1", [128, 512], BF16, at=offof(xqstg[1]))]
    Eb = [alloc("Eb0", [128, 1024], BF16, at=offof(xbuf[0])),
          alloc("Eb1", [128, 1024], BF16, at=offof(xbuf[1]))]
    Ofm = alloc("Ofm", [128, 8 * 256], BF16, at=offof(xntm))
    lnl = alloc("lnl", [128, 256], F32, at=offof(xqi))
    recl = alloc("recl", [128, 256], F32, at=offof(xqi) + 1024)
    attst = [alloc("attst0", [128, 512], BF16, at=offof(xqb[0])),
             alloc("attst1", [128, 512], BF16, at=offof(xnb[0]))]

    cursor[0] = PERS_END
    hB = alloc("hB", [128, 4 * 4096], F32)
    hqfm = alloc("hqfm", [128, 16384], BF16)
    terng = [alloc("terngA", [128, 4096], BF16), alloc("terngB", [128, 4096], BF16)]
    ternu = [alloc("ternuA", [128, 4096], BF16), alloc("ternuB", [128, 4096], BF16)]
    terndP = [alloc("terndA", [128, 2048], BF16), alloc("terndB", [128, 2048], BF16)]
    xqf = alloc("xqf", [128, 4096], BF16)
    hqbf = alloc("hqbf", [128, 4096], BF16)
    hqi = [alloc("hqi0", [128, 512], I32)]
    hqi.append(hqi[0])
    gbuf = alloc("gbuf", [128, 512], F32)
    dbuf = alloc("dbuf", [128, 1024], F32)
    obuf_s = alloc("obuf", [128, 1024], F32)
    obuf = [obuf_s, obuf_s]
    obuf2_s = alloc("obuf2", [128, 1024], F32)
    obuf2 = [obuf2_s, obuf2_s]
    xrel_s = alloc("xrel", [128, 1024], F32)
    xrel = [xrel_s, xrel_s]
    attb_s = alloc("attb", [128, 1024], BF16)
    attb = [attb_s, attb_s]
    C_END = cursor[0]
    assert max(AB_END, C_END) <= 229376, (PERS_END, AB_END, C_END)

    w_units = []
    w_pumped = [0]

    def pump_w(k, cap=None):
        lim = len(w_units) if cap is None else min(cap, len(w_units))
        while w_pumped[0] < lim and k > 0:
            w_units[w_pumped[0]]()
            w_pumped[0] += 1
            k -= 1

    wb_free = [None, None, None, None]
    wt_free = [None, None]
    w_ucount = [0]
    w_tcount = [0]

    def mk_passA(src_d_, col, kp, j):
        def unit():
            b = w_ucount[0] % 4
            w_ucount[0] += 1
            wws = [wb_free[b]] if wb_free[b] else []
            emit("sync", lambda e: e.dma_start(
                wbuf4[b][:], src_d_[128 * kp:128 * (kp + 1), 1024 * j:1024 * (j + 1)]),
                sem="SW%d" % b, n=16, waits=wws)
            emit("vector", lambda e: e.tensor_reduce(
                wsum[:, col:col + 1], wbuf4[b][:], AX, ALU.add,
                apply_absolute_value=True), sem="V", waits=["SW%d" % b])
            wb_free[b] = ("V", p.val("V"))
        return unit

    w_add_done = [0, 0]
    w_crep_v = [0]

    def mk_crep(ci, m):
        def unit():
            emit("vector", lambda e: e.tensor_reduce(
                acc2[:], wsum[:, 32 * m:32 * (m + 1)], AX, ALU.add), sem="V", waits=["V", "T"])
            emit("tensor", lambda e: e.matmul(
                pwq[0:1, ci:ci + 1], acc2[:], onesf[:], start=True, stop=True),
                sem="T", waits=["V"])
            emit("vector", lambda e: e.tensor_scalar(
                stage[0:1, 0:1], pwq[0:1, ci:ci + 1], 1.0 / (1024.0 * 4096.0), 1e-5,
                ALU.mult, ALU.max), sem="V", waits=["T", "V"])
            emit("vector", lambda e: e.tensor_scalar(stage[0:1, 2:3], stage[0:1, 0:1], 0.5, None, ALU.mult),
                 sem="V", waits=["V"])
            emit("vector", lambda e: e.reciprocal(stage[0:1, 3:4], stage[0:1, 0:1]), sem="V", waits=["V"])
            emit("tensor", lambda e: e.matmul(
                pwq[:, ci:ci + 2], onesrow_f[:], stage[0:1, 2:4], start=True, stop=True),
                sem="T", waits=["V"])
            emit("vector", lambda e: e.tensor_copy(crep[:, ci:ci + 2], pwq[:, ci:ci + 2]),
                 sem="V", waits=["T"])
            w_crep_v[0] = p.val("V")
        return unit

    def mk_passB(src_d_, dst_d_, ci, kp, j):
        def unit():
            b = w_ucount[0] % 4
            w_ucount[0] += 1
            tb_ = w_tcount[0] % 2
            w_tcount[0] += 1
            wws = [wb_free[b]] if wb_free[b] else []
            emit("sync", lambda e: e.dma_start(
                wbuf4[b][:], src_d_[128 * kp:128 * (kp + 1), 1024 * j:1024 * (j + 1)]),
                sem="SW%d" % b, n=16, waits=wws)
            emit("scalar", lambda e: e.activation(
                wsA[tb_][:], wbuf4[b][:], AF.Sign, bias=consts[:, 0:1], scale=crep[:, ci + 1:ci + 2]),
                waits=["SW%d" % b, ("V", max(w_add_done[tb_], w_crep_v[0]))])
            emit("scalar", lambda e: e.activation(
                wsB[tb_][:], wbuf4[b][:], AF.Sign, bias=consts[:, 1:2], scale=crep[:, ci + 1:ci + 2]),
                sem="SC")
            sc_after = p.val("SC")
            wb_free[b] = ("SC", sc_after)
            wws2 = [("SC", sc_after)] + ([wt_free[tb_]] if wt_free[tb_] else [])
            emit("vector", lambda e: e.tensor_add(wternb[tb_][:], wsA[tb_][:], wsB[tb_][:]),
                 sem="V", waits=wws2)
            w_add_done[tb_] = p.val("V")
            emit("gpsimd", lambda e: e.dma_start(
                dst_d_[128 * kp:128 * (kp + 1), 1024 * j:1024 * (j + 1)], wternb[tb_][:]),
                sem="GW%d" % tb_, n=16, waits=["V"])
            wt_free[tb_] = ("GW%d" % tb_, p.val("GW%d" % tb_))
        return unit

    if "W" in phases:
        for mi, (src_, dst_, ci_) in enumerate([(wgT_d, terng_d, 0), (wuT_d, ternu_d, 2)]):
            for kp in range(8):
                for j in range(4):
                    w_units.append(mk_passA(src_, 32 * mi + kp * 4 + j, kp, j))
        w_units.append(mk_crep(0, 0))
        w_units.append(mk_crep(2, 1))
        for kp in range(8):
            for j in range(4):
                w_units.append(mk_passB(wgT_d, terng_d, 0, kp, j))
        for kp in range(32):
            w_units.append(mk_passA(wdT_d, 64 + kp, kp, 0))
        w_units.append(mk_crep(4, 2))
        for kp in range(8):
            for j in range(4):
                w_units.append(mk_passB(wuT_d, ternu_d, 2, kp, j))
        for kp in range(32):
            w_units.append(mk_passB(wdT_d, ternd_d, 4, kp, 0))

    for b in (
        lambda e: e.memset(consts[:, 0:1], -0.5),
        lambda e: e.memset(consts[:, 1:2], 0.5),
        lambda e: e.memset(consts[:, 2:3], -20.0),
        lambda e: e.memset(consts[0:32, 3:4], -1.0),
        lambda e: e.memset(consts[32:64, 3:4], 1.0),
        lambda e: e.memset(consts[64:96, 3:4], -1.0),
        lambda e: e.memset(consts[96:128, 3:4], 1.0),
        lambda e: e.memset(onesf[:], 1.0),
        lambda e: e.memset(ones_bf[:], 1.0),
        lambda e: e.memset(onesrow_f[:], 1.0),
        lambda e: e.memset(onesrow_bf[:], 1.0),
        lambda e: e.memset(ident[:], 0.0),
        lambda e: e.affine_select(out=ident[:], in_=ident[:], compare_op=ALU.not_equal,
                                  fill=1.0, base=0, pattern=[[-1, 128]], channel_multiplier=1),
        lambda e: e.memset(maskAB[:], 0.0),
        lambda e: e.affine_select(out=maskAB[:, 0:256], in_=maskAB[:, 0:256], compare_op=ALU.is_ge,
                                  fill=-1e9, base=0, pattern=[[1, 256]], channel_multiplier=-1),
    ):
        emit("gpsimd", b)
    emit("gpsimd", lambda e: e.affine_select(out=maskAB[:, 256:512], in_=maskAB[:, 256:512],
                                             compare_op=ALU.is_ge, fill=-1e9, base=-128,
                                             pattern=[[1, 256]], channel_multiplier=-1), sem="GP")
    for (dlo, slo) in ((0, 32), (32, 0), (64, 96), (96, 64)):
        emit("gpsimd", (lambda dlo, slo: lambda e: e.dma_start(
            Prot[dlo:dlo + 32, :], ident[slo:slo + 32, :]))(dlo, slo), sem="dG", n=16)
    prot_dg = p.val("dG")

    for off in (0, 32, 64, 96):
        emit("sync", (lambda off: lambda e: e.dma_start(fbuf[off:off + 32, :], invf_hi_d[:]))(off),
             sem="dS", n=16)
        emit("sync", (lambda off: lambda e: e.dma_start(flbuf[off:off + 32, :], invf_lo_d[:]))(off),
             sem="dS", n=16)
    emit("sync", lambda e: e.dma_start(bpf[0:1, :], bproj_d[:]), sem="dS", n=16)
    emit("vector", lambda e: e.tensor_copy(bproj_bf[0:1, :], bpf[0:1, :]), waits=["dS"])
    def emit_wrap(dst_fn, src_fn, n, ta, tb, sem=None):
        emit("vector", lambda e: e.tensor_scalar(ta[:, 0:n], src_fn(), -PI, None, ALU.is_lt),
             sem="V", waits=["V"])
        emit("vector", lambda e: e.tensor_scalar(tb[:, 0:n], src_fn(), PI, None, ALU.is_gt),
             sem="V", waits=["V"])
        emit("vector", lambda e: e.tensor_sub(ta[:, 0:n], ta[:, 0:n], tb[:, 0:n]),
             sem="V", waits=["V"])
        emit("vector", lambda e: e.scalar_tensor_tensor(
            dst_fn(), ta[:, 0:n], TWO_PI, src_fn(), ALU.mult, ALU.add), sem=(sem or "V"), waits=["V"])

    emit("vector", lambda e: e.memset(utab[:, 0:1], 0.0), sem="V")
    emit("vector", lambda e: e.tensor_add(utab[:, 1:2], fbuf[:], flbuf[:]), sem="V", waits=["V"])
    sz = 1
    while sz < S:
        if sz >= 2:
            hs_ = sz // 2
            emit("vector", (lambda hs_: lambda e: e.tensor_scalar(
                utmp[:, 0:1], utab[:, hs_:hs_ + 1], utab[:, hs_:hs_ + 1], None, ALU.add))(hs_),
                sem="V", waits=["V"])
            emit_wrap((lambda sz: lambda: utab[:, sz:sz + 1])(sz), lambda: utmp[:, 0:1],
                      1, ts1, ts2)
        n = min(sz, S - sz)
        emit("vector", (lambda sz, n: lambda e: e.tensor_scalar(
            utmp[:, 0:n], utab[:, 0:n], utab[:, sz:sz + 1], None, ALU.add))(sz, n),
            sem="V", waits=["V"])
        emit_wrap((lambda sz, n: lambda: utab[:, sz:sz + n])(sz, n),
                  (lambda n: lambda: utmp[:, 0:n])(n), n, ts1, ts2, sem="V")
        pump_w(1, cap=64)
        sz *= 2
    emit("scalar", lambda e: e.activation(ts1[:, 0:S], utab[:], AF.Sin, bias=0.0, scale=1.0),
         sem="SC", waits=["V"])
    emit("vector", lambda e: e.tensor_scalar(sinS[:], ts1[:, 0:S], consts[:, 3:4], None, ALU.mult),
         waits=["SC", "GP"])
    emit("vector", lambda e: e.tensor_scalar(ts2[:, 0:S], utab[:], HALF_PI, None, ALU.add), sem="V")
    emit_wrap(lambda: utmp2[:, 0:S], lambda: ts2[:, 0:S], S, ts1, utab, sem="V")
    emit("scalar", lambda e: e.activation(cosF[:], utmp2[:, 0:S], AF.Sin, bias=0.0, scale=1.0),
         sem="SC", waits=["V"])
    pump_w(3, cap=64)

    t_tbl_sc = p.val("SC")

    if True:
      with (
        nc.psum_tensor("pqkv0", [128, 512], F32) as pq0,
        nc.psum_tensor("pqkv1", [128, 512], F32) as pq1,
        nc.psum_tensor("ptr0", [128, 1024], BF16) as pt0,
        nc.psum_tensor("ptr1", [128, 1024], BF16) as pt1,
        nc.psum_tensor("pwq", [128, 8], F32) as pwq,
        nc.psum_tensor("prot0", [128, 512], F32) as pr0,
        nc.psum_tensor("prot1", [128, 512], F32) as pr1,
      ):
        pqkv = [pq0, pq1]
        ptr = [pt0, pt1]
        xb_free = [[], []]
        ptr_free = [("V", 0), ("V", 0)]
        xqs_free = [("GA0", 0), ("GA1", 0)]
        xnfm_r = xnfm[:].rearrange("p (c tb w) -> p c tb w", c=NCH, tb=NT)
        xq_dst = xqfm_d[:].rearrange("(c p) (tb w) -> p c tb w", p=128, w=128)
        A1R = range(A1N) if ("A" in phases or "1" in phases) else []

        def dma_x(t, par):
            emit("sync", (lambda t, par: lambda e: e.dma_start(
                xbuf[par][:], x_d[128 * t:128 * (t + 1), :]))(t, par),
                sem="SA%d" % par, n=16, waits=list(xb_free[par]))
        for t in A1R:
            par = t % 2
            dma_x(t, par)
            emit("vector", (lambda t, par: lambda e: e.reduce_sum(
                ast[:, t:t + 1], xbuf[par][:], AX))(t, par), sem="V", waits=["SA%d" % par])
            emit("scalar", (lambda t, par: lambda e: e.activation(
                xsq[:], xbuf[par][:], AF.Square, bias=0.0, scale=1.0,
                accum_out=ast[:, 16 + t:17 + t]))(t, par), sem="SC", waits=["SA%d" % par])
            xb_free[par] = [("V", p.val("V")), ("SC", p.val("SC"))]
            pump_w(2)
        if A1N:
            emit("vector", lambda e: e.tensor_scalar(ast[:, 32:48], ast[:, 0:16], 1.0 / D, None, ALU.mult),
                 sem="V", waits=["V", "SC"])
            emit("vector", lambda e: e.tensor_scalar(ast[:, 48:64], ast[:, 16:32], 1.0 / D, 1e-5, ALU.mult, ALU.add),
                 sem="V", waits=["V"])
            emit("vector", lambda e: e.tensor_mul(ast[:, 64:80], ast[:, 32:48], ast[:, 32:48]),
                 sem="V", waits=["V"])
            emit("vector", lambda e: e.tensor_sub(ast[:, 80:96], ast[:, 48:64], ast[:, 64:80]),
                 sem="V", waits=["V"])
            emit("scalar", lambda e: e.sqrt(ast[:, 96:112], ast[:, 80:96]), sem="SC", waits=["V"])
            emit("vector", lambda e: e.reciprocal(ast[:, 112:128], ast[:, 96:112]), sem="V", waits=["SC", "V"])
            emit("vector", lambda e: e.tensor_mul(ast[:, 128:144], ast[:, 32:48], ast[:, 112:128]),
                 sem="V", waits=["V"])
            pump_w(2)
        for t in A1R:
            par = t % 2
            dma_x(t, par)
            emit("vector", (lambda t, par: lambda e: e.tensor_scalar(
                xsq[:], xbuf[par][:], ast[:, 112 + t:113 + t], ast[:, 128 + t:129 + t],
                ALU.mult, ALU.subtract))(t, par), sem="V", waits=["SA%d" % par])
            emit("vector", (lambda t: lambda e: e.tensor_reduce(
                ast[:, 144 + t:145 + t], xsq[:], AX, ALU.max,
                apply_absolute_value=True))(t), sem="V", waits=["V"])
            xb_free[par] = [("V", p.val("V"))]
            pump_w(3)
        if A1N:
            emit("vector", lambda e: e.tensor_scalar(ast[:, 160:176], ast[:, 144:160], 1e-5, None, ALU.max),
                 sem="V", waits=["V"])
            emit("vector", lambda e: e.tensor_scalar(cx[:, 0:NT], ast[:, 160:176], 1.0 / 127.0, None, ALU.mult),
                 sem="V", waits=["V"])
            emit("vector", lambda e: e.reciprocal(ast[:, 176:192], ast[:, 160:176]), sem="V", waits=["V"])
            emit("vector", lambda e: e.tensor_scalar(ast[:, 192:208], ast[:, 176:192], 127.0, None, ALU.mult),
                 sem="V", waits=["V"])
            emit("vector", lambda e: e.tensor_mul(ast[:, 208:224], ast[:, 112:128], ast[:, 192:208]),
                 sem="V", waits=["V"])
            emit("vector", lambda e: e.tensor_mul(ast[:, 224:240], ast[:, 128:144], ast[:, 192:208]),
                 sem="V", waits=["V"])
            pump_w(2)
        for t in A1R:
            par = t % 2
            dma_x(t, par)
            emit("vector", (lambda t, par: lambda e: e.tensor_scalar(
                xnb[par][:], xbuf[par][:], ast[:, 112 + t:113 + t], ast[:, 128 + t:129 + t],
                ALU.mult, ALU.subtract))(t, par), sem="V", waits=["SA%d" % par])
            for c in range(NCH):
                emit("tensor", (lambda c, par: lambda e: e.transpose(
                    ptr[par][:, 128 * c:128 * (c + 1)], xnb[par][:, 128 * c:128 * (c + 1)], ident[:]))(c, par),
                    waits=(["V", ptr_free[par]] if c == 0 else []),
                    sem=("T" if c == NCH - 1 else None))
            emit("vector", (lambda t, par: lambda e: e.tensor_scalar(
                xqi[:], xbuf[par][:], ast[:, 208 + t:209 + t], ast[:, 224 + t:225 + t],
                ALU.mult, ALU.subtract))(t, par))
            emit("vector", (lambda par: lambda e: e.tensor_scalar(
                xqb[par][:], xqi[:], 1.0, None, ALU.mult))(par), sem="V")
            xb_free[par] = [("V", p.val("V"))]
            emit("vector", (lambda t, par: lambda e: e.tensor_copy(
                xnfm_r[:, 0:4, t, :], ptr[par][:, 0:512].rearrange("p (c w) -> p c w", c=4)))(t, par),
                waits=["T"])
            emit("vector", (lambda t, par: lambda e: e.tensor_copy(
                xnfm_r[:, 4:8, t, :], ptr[par][:, 512:1024].rearrange("p (c w) -> p c w", c=4)))(t, par),
                sem="V")
            for c in range(NCH):
                emit("tensor", (lambda c, par: lambda e: e.transpose(
                    ptr[par][:, 128 * c:128 * (c + 1)], xqb[par][:, 128 * c:128 * (c + 1)], ident[:]))(c, par),
                    waits=(["V"] if c == 0 else []),
                    sem=("T" if c == NCH - 1 else None))
            if "xq" not in A1SKIP:
                emit("scalar", (lambda par: lambda e: e.activation(
                    xqstg[par][:], ptr[par][:], AF.Copy, bias=0.0, scale=1.0))(par),
                    sem="SC", waits=["T", xqs_free[par]])
                ptr_free[par] = ("SC", p.val("SC"))
                emit("gpsimd", (lambda t, par: lambda e: e.dma_start(
                    xq_dst[:, :, t, :], xqstg[par][:].rearrange("p (c w) -> p c w", c=NCH)))(t, par),
                    sem="GA%d" % par, n=16, waits=["SC"])
                xqs_free[par] = ("GA%d" % par, p.val("GA%d" % par))
            else:
                ptr_free[par] = ("T", p.val("T"))
            pump_w(3)

        a1_dg = p.val("dG")
        wq_free = [0, 0]
        pq_free = [0, 0]
        qs_free = [0, 0]
        vt_free = [0, 0]
        ptr_free = [("SC", p.val("SC")), ("SC", p.val("SC"))]
        prot_ps = [pr0, pr1]
        rotps_free = [0, 0]
        rp_ctr = [0]
        s1q = []
        s2q = []

        def emit_rot(ent):
            (qpar, tb2, dest2, cch2, cast_v) = ent
            rpar = rp_ctr[0] % 2
            rp_ctr[0] += 1
            emit("tensor", (lambda qpar, rpar: lambda e: e.matmul(
                prot_ps[rpar][:], Prot[:], qsb[qpar][:], start=True, stop=True))(qpar, rpar),
                waits=[("V", cast_v), ("V", rotps_free[rpar]), ("dG", prot_dg)], sem="T")
            s2q.append((qpar, tb2, dest2, cch2, rpar, p.val("T")))

        def emit_ropeV(ent):
            (qpar, tb2, dest2, cch2, rpar, rtv) = ent
            emit("vector", (lambda qpar, tb2: lambda e: e.tensor_mul(
                tP[:], qsb[qpar][:], cosF[:, 512 * tb2:512 * (tb2 + 1)]))(qpar, tb2),
                waits=[("SC", t_tbl_sc)])
            emit("vector", (lambda rpar, tb2: lambda e: e.tensor_mul(
                tR[:], prot_ps[rpar][:], sinS[:, 512 * tb2:512 * (tb2 + 1)]))(rpar, tb2),
                waits=[("T", rtv)])
            emit("vector", (lambda dest2, cch2, tb2: lambda e: e.tensor_add(
                dest2[:, S * cch2 + 512 * tb2: S * cch2 + 512 * (tb2 + 1)], tP[:], tR[:]))(dest2, cch2, tb2),
                sem="V")
            rotps_free[rpar] = p.val("V")

        def rope_push(ent):
            s1q.append(ent)
            if len(s1q) >= 2:
                emit_rot(s1q.pop(0))
            if len(s2q) >= 2:
                emit_ropeV(s2q.pop(0))

        def rope_flush():
            while s1q:
                emit_rot(s1q.pop(0))
            while s2q:
                emit_ropeV(s2q.pop(0))
        vdst = vtm[:].rearrange("p (tk col) -> p tk col", col=D)
        w_flush_waits = []
        for dc in (range(24) if ("A" in phases or "2" in phases) else []):
            if dc == 16:
                rope_flush()
                pump_w(1 << 30)
                w_flush_waits = [("SC", p.val("SC")), ("GW0", p.val("GW0")),
                                 ("GW1", p.val("GW1"))]
            else:
                pump_w(4)
            par = dc % 2
            emit("sync", (lambda dc, par: lambda e: e.dma_start(
                wqb[par][:], wqkv_d[128 * dc:128 * (dc + 1), :]))(dc, par),
                sem="SA%d" % par, n=16, waits=[("T", wq_free[par])])
            emit("vector", (lambda par: lambda e: e.tensor_copy(wqbf[par][:], wqb[par][:]))(par),
                 sem="V", waits=["SA%d" % par])
            cast_done = p.val("V")
            for tb in range(4):
                ppar = (dc * 4 + tb) % 2
                for di in range(NCH):
                    emit("tensor", (lambda par, ppar, di, tb: lambda e: e.matmul(
                        pqkv[ppar][:], wqbf[par][:, 128 * di:128 * (di + 1)],
                        xnfm[:, S * di + 512 * tb: S * di + 512 * (tb + 1)],
                        start=(di == 0), stop=(di == NCH - 1)))(par, ppar, di, tb),
                        waits=([("V", cast_done), ("V", pq_free[ppar])] if di == 0 else []),
                        sem=("T" if di == NCH - 1 else None))
                if dc < 16:
                    qpar = (dc * 4 + tb) % 4
                    dest = qfm if dc < 8 else kfm
                    cch = dc % 8
                    emit("vector", (lambda ppar, qpar: lambda e: e.tensor_copy(
                        qsb[qpar][:], pqkv[ppar][:]))(ppar, qpar),
                        sem="V", waits=["T"])
                    pq_free[ppar] = p.val("V")
                    rope_push((qpar, tb, dest, cch, p.val("V")))
                else:
                    vch = dc - 16
                    vpar = (dc * 4 + tb) % 2
                    emit("vector", (lambda ppar, vpar: lambda e: e.tensor_copy(
                        vtmp[vpar][:], pqkv[ppar][:]))(ppar, vpar), sem="V",
                        waits=["T", ("T", vt_free[vpar]), "GA0", "GA1"])
                    pq_free[ppar] = p.val("V")
                    for s_ in range(4):
                        emit("tensor", (lambda vpar, s_: lambda e: e.transpose(
                            ptr[vpar][:, 128 * s_:128 * (s_ + 1)], vtmp[vpar][:, 128 * s_:128 * (s_ + 1)],
                            ident[:]))(vpar, s_),
                            waits=(["V", ptr_free[vpar]] if s_ == 0 else []),
                            sem=("T" if s_ == 3 else None))
                    vt_free[vpar] = p.val("T")
                    emit("vector", (lambda vpar, vch, tb: lambda e: e.tensor_copy(
                        vdst[:, 4 * tb:4 * (tb + 1), 128 * vch:128 * (vch + 1)],
                        ptr[vpar][:, 0:512].rearrange("p (s w) -> p s w", s=4)))(vpar, vch, tb),
                        sem="V", waits=["T"] + w_flush_waits)
                    w_flush_waits = []
                    ptr_free[vpar] = ("V", p.val("V"))
            wq_free[par] = p.val("T")

        if DBGA2:
            emit("gpsimd", lambda e: e.dma_start(dbgq_d[:], qfm[:]), sem="dG", n=16,
                 waits=["V", ("T", p.val("T"))])
            emit("gpsimd", lambda e: e.dma_start(dbgk_d[:], kfm[:]), sem="dG", n=16)
            emit("gpsimd", lambda e: e.dma_start(dbgv_d[:], vtm[:]), sem="dG", n=16)
        rope_flush()
        wp_t = p.val("T")
        for c in (range(NCH) if ("A" in phases or "2" in phases) else []):
            emit("sync", (lambda c, wp_t: lambda e: e.dma_start(
                wqb[0][:], wproj_d[128 * c:128 * (c + 1), :]))(c, wp_t), sem="SA0", n=16,
                waits=[("T", wp_t), ("V", p.val("V"))])
            emit("vector", (lambda c: lambda e: e.tensor_copy(
                wprojb[:, D * c:D * (c + 1)], wqb[0][:]))(c), sem="V", waits=["SA0"])
        pump_w(1 << 30)

    if "B" in phases:
      with (
        nc.psum_tensor("pSh0", [128, 1024], F32) as pSh0,
        nc.psum_tensor("pSh1", [128, 1024], F32) as pSh1,
        nc.psum_tensor("pO", [128, 256], F32) as pO,
        nc.psum_tensor("pl", [128, 256], F32) as pl,
        nc.psum_tensor("pproj0", [128, 512], F32) as pp0,
        nc.psum_tensor("pproj1", [128, 512], F32) as pp1,
      ):
        pproj = [pp0, pp1]
        att_free = [0, 0]
        for b in range(NB):
            ntk = 2 * (b + 1)
            for cch in range(NCH):
                sc_pair_start = p.val("SC")
                v_norm_done = p.val("V")
                sc_bank_done = [sc_pair_start, sc_pair_start]
                nsteps = ntk // 2

                def emit_qk(jj):
                    bank = jj % 2
                    for j in (2 * jj, 2 * jj + 1):
                        qtr = j % 4
                        for (hs, psum, tpos) in ((0, pSh0, (0, 0)), (64, pSh1, (64, 0))):
                            w = [("SC", sc_bank_done[bank])] if j == 2 * jj and hs == 0 else []
                            emit("tensor", (lambda j, qtr, hs, psum, tpos, cch, b: lambda e: e.matmul(
                                psum[:, 256 * qtr:256 * (qtr + 1)],
                                kfm[hs:hs + 64, S * cch + 128 * j: S * cch + 128 * (j + 1)],
                                qfm[hs:hs + 64, S * cch + 256 * b: S * cch + 256 * (b + 1)],
                                start=True, stop=True,
                                tile_position=tpos))(j, qtr, hs, psum, tpos, cch, b),
                                waits=w, sem=("T" if hs == 64 else None))

                def emit_exp(jj):
                    j1 = 2 * jj + 1
                    qtr0 = (2 * jj) % 4
                    bank = jj % 2
                    tw = [("T", qk_done[jj]), ("T", avl_done.get(jj - 2, 0))]
                    if j1 == ntk - 1:
                        for psum in (pSh0, pSh1):
                            emit("vector", (lambda psum, qtr0: lambda e: e.tensor_tensor(
                                psum[:, 256 * qtr0:256 * qtr0 + 512],
                                psum[:, 256 * qtr0:256 * qtr0 + 512], maskAB[:], ALU.add))(psum, qtr0),
                                sem="V", waits=tw)
                    for (psum, ebuf) in ((pSh0, Eb[0]), (pSh1, Eb[1])):
                        emit("scalar", (lambda psum, ebuf, qtr0, bank: lambda e: e.activation(
                            ebuf[:, 512 * bank:512 * (bank + 1)], psum[:, 256 * qtr0:256 * qtr0 + 512],
                            AF.Exp, bias=consts[:, 2:3], scale=0.125))(psum, ebuf, qtr0, bank),
                            sem="SC", waits=tw + ["V"])
                    sc_bank_done[bank] = p.val("SC")
                    exp_done[jj] = p.val("SC")

                def emit_avl(jj):
                    bank = jj % 2
                    for j in (2 * jj, 2 * jj + 1):
                        qtr = j % 4
                        eoff = 256 * (qtr % 2) + 512 * bank
                        first = (j == 0)
                        last = (j == ntk - 1)
                        for (hs, ebuf, tpos) in ((0, Eb[0], (0, 0)), (64, Eb[1], (0, 64))):
                            w = []
                            if hs == 0:
                                w = [("SC", exp_done[jj])]
                                if first:
                                    w.append(("V", v_norm_done))
                            emit("tensor", (lambda j, hs, ebuf, eoff, tpos, first, last, cch: lambda e: e.matmul(
                                pO[hs:hs + 64, :],
                                vtm[:, D * j + 64 * (2 * cch + hs // 64): D * j + 64 * (2 * cch + hs // 64) + 64],
                                ebuf[:, eoff:eoff + 256],
                                start=first, stop=last,
                                tile_position=tpos))(j, hs, ebuf, eoff, tpos, first, last, cch),
                                waits=w)
                        for (hs, ebuf, tpos) in ((0, Eb[0], (0, 0)), (64, Eb[1], (0, 64))):
                            emit("tensor", (lambda hs, ebuf, eoff, tpos, first, last: lambda e: e.matmul(
                                pl[hs:hs + 64, :], ones_bf[:], ebuf[:, eoff:eoff + 256],
                                start=first, stop=last,
                                tile_position=tpos))(hs, ebuf, eoff, tpos, first, last),
                                sem=("T" if hs == 64 else None))
                    avl_done[jj] = p.val("T")

                qk_done = {}
                exp_done = {}
                avl_done = {}
                for jj in range(nsteps):
                    emit_qk(jj)
                    qk_done[jj] = p.val("T")
                    if jj >= 1:
                        emit_exp(jj - 1)
                    if jj >= 2:
                        emit_avl(jj - 2)
                emit_exp(nsteps - 1)
                if nsteps >= 2:
                    emit_avl(nsteps - 2)
                emit_avl(nsteps - 1)
                emit("scalar", lambda e: e.activation(lnl[:], pl[:], AF.Ln, bias=0.0, scale=1.0),
                     sem="SC", waits=["T"])
                emit("scalar", lambda e: e.activation(recl[:], lnl[:], AF.Exp, bias=0.0, scale=-1.0),
                     sem="SC", waits=["SC"])
                emit("vector", (lambda cch: lambda e: e.tensor_tensor(
                    Ofm[:, 256 * cch:256 * (cch + 1)], pO[:], recl[:], ALU.mult))(cch),
                    sem="V", waits=["SC"])
            if DBGA2:
                emit("gpsimd", (lambda b: lambda e: e.dma_start(
                    dbgo_d[128 * b:128 * (b + 1), :], Ofm[:]))(b), sem="dG", n=16, waits=["V"])
            for m2 in range(2):
                for nn in range(2):
                    for c2 in range(NCH):
                        emit("tensor", (lambda m2, nn, c2: lambda e: e.matmul(
                            pproj[nn][:], Ofm[:, 256 * c2 + 128 * m2: 256 * c2 + 128 * (m2 + 1)],
                            wprojb[:, D * c2 + 512 * nn: D * c2 + 512 * (nn + 1)],
                            start=(c2 == 0), stop=False))(m2, nn, c2),
                            waits=(["V"] if c2 == 0 else []))
                    emit("tensor", (lambda nn: lambda e: e.matmul(
                        pproj[nn][:], onesrow_bf[:], bproj_bf[0:1, 512 * nn:512 * (nn + 1)],
                        start=False, stop=True))(nn), sem="T")
                    emit("vector", (lambda nn: lambda e: e.tensor_copy(
                        attst[nn][:], pproj[nn][:]))(nn), sem="V",
                        waits=["T", "GA%d" % nn])
                    emit("gpsimd", (lambda b, m2, nn: lambda e: e.dma_start(
                        attn_d[256 * b + 128 * m2: 256 * b + 128 * (m2 + 1), 512 * nn:512 * (nn + 1)],
                        attst[nn][:]))(b, m2, nn), sem="GA%d" % nn, n=16, waits=["V"])

    p.barrier()

    if "C" in phases:
      with (
        nc.psum_tensor("pg0", [128, 512], F32) as pg0,
        nc.psum_tensor("pg1", [128, 512], F32) as pg1,
        nc.psum_tensor("pu0", [128, 512], F32) as pu0,
        nc.psum_tensor("pu1", [128, 512], F32) as pu1,
        nc.psum_tensor("pd0", [128, 512], F32) as pd0,
        nc.psum_tensor("pd1", [128, 512], F32) as pd1,
        nc.psum_tensor("ptrh", [128, 1024], BF16) as ptrh,
      ):
        pg = [pg0, pg1]
        pu = [pu0, pu1]
        pd = [pd0, pd1]
        xq_src = xqfm_d[:].rearrange("(c p) s -> p c s", p=128)
        tg_src = terng_d[:].rearrange("(c p) q -> p c q", p=128)
        tu_src = ternu_d[:].rearrange("(c p) q -> p c q", p=128)
        hqfm_w = hqfm[:].rearrange("p (j mm w) -> p j mm w", mm=4, w=128)
        ptrh_r = ptrh[:].rearrange("p (j w) -> p j w", w=128)
        tern_free = [0, 0]
        terndP_free = [0, 0]
        t_gate_done = 0
        x_free = [0, 0]
        o_free = [0, 0]
        hq_done = 0
        for M in range(CN):
            emit("sync", (lambda M: lambda e: e.dma_start(
                xqf[:].rearrange("p (c w) -> p c w", c=NCH),
                xq_src[:, :, 512 * M:512 * (M + 1)]))(M),
                sem="XQ", n=16, waits=[("T", (t_gate_done if M > 0 else p.val("T")))])
            for m in range(4):
                emit("vector", (lambda M, m: lambda e: e.tensor_scalar(
                    hstats[:, 40 + m:41 + m], cx[:, 4 * M + m:4 * M + m + 1],
                    crep[:, 0:1], None, ALU.mult))(M, m), sem="V", waits=["V"])
                emit("vector", (lambda M, m: lambda e: e.tensor_scalar(
                    hstats[:, 44 + m:45 + m], cx[:, 4 * M + m:4 * M + m + 1],
                    crep[:, 2:3], None, ALU.mult))(M, m), sem="V", waits=["V"])
            gu_done = [p.val("V"), p.val("V")]
            for ic in (range(8) if "gu" not in CSKIP else []):
                tpar = ic % 2
                emit("sync", (lambda ic, tpar: lambda e: e.dma_start(
                    terng[tpar][:].rearrange("p (c w) -> p c w", c=NCH),
                    tg_src[:, :, 512 * ic:512 * (ic + 1)]))(ic, tpar),
                    sem="TG%d" % tpar, n=16, waits=[("V", tern_free[tpar])])
                emit("sync", (lambda ic, tpar: lambda e: e.dma_start(
                    ternu[tpar][:].rearrange("p (c w) -> p c w", c=NCH),
                    tu_src[:, :, 512 * ic:512 * (ic + 1)]))(ic, tpar),
                    sem="TG%d" % tpar, n=16)
                for m in range(4):
                    gpar = (ic * 4 + m) % 2
                    for c in range(NCH):
                        emit("tensor", (lambda c, m, gpar, tpar: lambda e: e.matmul(
                            pg[gpar][:], xqf[:, 512 * c + 128 * m: 512 * c + 128 * (m + 1)],
                            terng[tpar][:, 512 * c:512 * (c + 1)],
                            start=(c == 0), stop=(c == NCH - 1)))(c, m, gpar, tpar),
                            waits=(["TG%d" % tpar, "XQ", ("V", gu_done[gpar])] if c == 0 else []))
                    for c in range(NCH):
                        emit("tensor", (lambda c, m, gpar, tpar: lambda e: e.matmul(
                            pu[gpar][:], xqf[:, 512 * c + 128 * m: 512 * c + 128 * (m + 1)],
                            ternu[tpar][:, 512 * c:512 * (c + 1)],
                            start=(c == 0), stop=(c == NCH - 1)))(c, m, gpar, tpar),
                            sem=("T" if c == NCH - 1 else None))
                    emit("scalar", (lambda m, gpar: lambda e: e.activation(
                        gbuf[:], pg[gpar][:], AF.Silu, bias=0.0,
                        scale=hstats[:, 40 + m:41 + m]))(m, gpar), sem="SC", waits=["T", "V"])
                    emit("vector", (lambda m, ic, gpar: lambda e: e.tensor_tensor(
                        hB[:, 4096 * m + 512 * ic: 4096 * m + 512 * (ic + 1)], gbuf[:],
                        pu[gpar][:], ALU.mult))(m, ic, gpar), waits=["SC"])
                    emit("vector", (lambda m, ic: lambda e: e.tensor_reduce(
                        hstats[:, 8 * m + ic: 8 * m + ic + 1],
                        hB[:, 4096 * m + 512 * ic: 4096 * m + 512 * (ic + 1)], AX, ALU.max,
                        apply_absolute_value=True))(m, ic), sem="V")
                    gu_done[gpar] = p.val("V")
                tern_free[tpar] = p.val("V")
            t_gate_done = p.val("T")

            def emit_tail(m):
                nonlocal hq_done
                if "tail" in CSKIP:
                    return
                emit("vector", (lambda m: lambda e: e.tensor_reduce(
                    hstats[:, 32 + m:33 + m], hstats[:, 8 * m:8 * m + 8], AX, ALU.max))(m),
                    sem="V", waits=["V"])
                emit("vector", (lambda m: lambda e: e.tensor_tensor(
                    hstats[:, 36:37], hstats[:, 32 + m:33 + m], hstats[:, 44 + m:45 + m],
                    ALU.mult))(m), sem="V", waits=["V"])
                emit("vector", lambda e: e.tensor_scalar(hstats[:, 37:38], hstats[:, 36:37], 1e-5, None, ALU.max),
                     sem="V", waits=["V"])
                emit("vector", lambda e: e.reciprocal(hstats[:, 38:39], hstats[:, 37:38]),
                     sem="V", waits=["V"])
                emit("vector", (lambda m: lambda e: e.tensor_tensor(
                    hstats[:, 36:37], hstats[:, 38:39], hstats[:, 44 + m:45 + m], ALU.mult))(m),
                    sem="V", waits=["V"])
                emit("vector", lambda e: e.tensor_scalar(hstats[:, 39:40], hstats[:, 36:37], 127.0, None, ALU.mult),
                     sem="V", waits=["V"])
                emit("vector", (lambda m: lambda e: e.tensor_scalar(
                    hstats[:, 48 + m:49 + m], hstats[:, 37:38], 1.0 / 127.0, None, ALU.mult))(m),
                    sem="V", waits=["V"])
                emit("vector", (lambda m: lambda e: e.tensor_scalar(
                    hstats[:, 48 + m:49 + m], hstats[:, 48 + m:49 + m], crep[:, 4:5], None, ALU.mult))(m),
                    sem="V", waits=["V"])
                for q in range(8):
                    qp = q % 2
                    emit("vector", (lambda m, q, qp: lambda e: e.tensor_scalar(
                        hqi[qp][:], hB[:, 4096 * m + 512 * q: 4096 * m + 512 * (q + 1)],
                        hstats[:, 39:40], None, ALU.mult))(m, q, qp),
                        waits=(["V"] if q == 0 else []))
                    emit("vector", (lambda q, qp: lambda e: e.tensor_scalar(
                        hqbf[:, 512 * q:512 * (q + 1)], hqi[qp][:], 1.0, None, ALU.mult))(q, qp),
                        sem="V")
                for grp in range(4):
                    for i2 in range(8):
                        emit("tensor", (lambda grp, i2: lambda e: e.transpose(
                            ptrh[:, 128 * i2:128 * (i2 + 1)],
                            hqbf[:, 1024 * grp + 128 * i2: 1024 * grp + 128 * (i2 + 1)], ident[:]))(grp, i2),
                            waits=((["V", ("T", hq_done)] if grp == 0 else ["V"]) if i2 == 0 else []),
                            sem=("T" if i2 == 7 else None))
                    emit("vector", (lambda grp, m: lambda e: e.tensor_copy(
                        hqfm_w[:, 8 * grp:8 * (grp + 1), m, :], ptrh_r[:]))(grp, m),
                        sem="V", waits=["T"])
                hq_done = p.val("T")
                if DBGC:
                    emit("gpsimd", (lambda M, m: lambda e: e.dma_start(
                        dbghq_d[512 * M + 128 * m: 512 * M + 128 * (m + 1), :], hqbf[:]))(M, m),
                        sem="dG", n=16, waits=["V"])
                    emit("gpsimd", (lambda M, m: lambda e: e.dma_start(
                        dbghst_d[128 * (4 * M + m): 128 * (4 * M + m + 1), :], hstats[:]))(M, m),
                        sem="dG", n=16)
                    emit("gpsimd", (lambda M, m: lambda e: e.dma_start(
                        dbghqf_d[128 * (4 * M + m): 128 * (4 * M + m + 1), :], hqfm[:]))(M, m),
                        sem="dG", n=16, waits=["V"])
            emit_tail(0)
            emit_tail(1)
            sc_now = p.val("SC")
            tdsrc = ternd_d[:].rearrange("(g p) q -> p g q", p=128)
            for half in ((0, 1) if ("tail" not in CSKIP and "down" not in CSKIP) else ()):
                v_hq_done = p.val("V")
                pdq = [[pd0, pd1], [pg0, pg1]]
                for j2 in range(16):
                    if half == 0 and j2 == 1:
                        emit_tail(2)
                    if half == 0 and j2 == 8:
                        emit_tail(3)
                    jp = j2 % 2
                    emit("sync", (lambda j2, jp: lambda e: e.dma_start(
                        terndP[jp][:].rearrange("p (g q) -> p g q", g=2),
                        tdsrc[:, 2 * j2:2 * j2 + 2, :]))(j2, jp),
                        sem="TD%d" % jp, n=16, waits=[("T", terndP_free[jp])])
                    for jsub in (0, 1):
                        j = 2 * j2 + jsub
                        for m2 in (0, 1):
                            m = 2 * half + m2
                            for nn in range(2):
                                w = []
                                if jsub == 0 and m2 == 0 and nn == 0:
                                    w = ["TD%d" % jp]
                                    if j2 == 0:
                                        w += [("V", v_hq_done), ("SC", sc_now)]
                                emit("tensor", (lambda j, jp, jsub, m, m2, nn: lambda e: e.matmul(
                                    pdq[m2][nn][:], hqfm[:, 512 * j + 128 * m: 512 * j + 128 * (m + 1)],
                                    terndP[jp][:, 1024 * jsub + 512 * nn: 1024 * jsub + 512 * (nn + 1)],
                                    start=(j == 0), stop=(j == 31)))(j, jp, jsub, m, m2, nn),
                                    waits=w,
                                    sem=("T" if (jsub == 1 and m2 == 1 and nn == 1) else None))
                    terndP_free[jp] = p.val("T")
                for m2 in (0, 1):
                    m = 2 * half + m2
                    opar = m % 2
                    for nn in range(2):
                        emit("vector", (lambda m, m2, nn: lambda e: e.tensor_scalar(
                            dbuf[:, 512 * nn:512 * (nn + 1)], pdq[m2][nn][:], hstats[:, 48 + m:49 + m], None,
                            ALU.mult))(m, m2, nn), sem="V", waits=["T", "V"])
                    emit("sync", (lambda M, m, opar: lambda e: e.dma_start(
                        xrel[opar][:], x_d[512 * M + 128 * m: 512 * M + 128 * (m + 1), :]))(M, m, opar),
                        sem="XR", n=16, waits=[("V", x_free[opar])])
                    emit("sync", (lambda M, m, opar: lambda e: e.dma_start(
                        attb[opar][:], attn_d[512 * M + 128 * m: 512 * M + 128 * (m + 1), :]))(M, m, opar),
                        sem="XR", n=16)
                    emit("vector", (lambda opar: lambda e: e.tensor_add(
                        obuf[opar][:], dbuf[:], xrel[opar][:]))(opar),
                        waits=["XR"])
                    emit("vector", (lambda opar: lambda e: e.tensor_add(
                        obuf2[opar][:], obuf[opar][:], attb[opar][:]))(opar), sem="V",
                        waits=["GO"])
                    x_free[0] = p.val("V")
                    x_free[1] = x_free[0]
                    emit("gpsimd", (lambda M, m, opar: lambda e: e.dma_start(
                        out_d[512 * M + 128 * m: 512 * M + 128 * (m + 1), :], obuf2[opar][:]))(M, m, opar),
                        sem="GO", n=16, waits=["V"])
            gu_done_next = p.val("V")
    final_waits = [(s, p.val(s)) for s in SEMNAMES
                   if s not in ("T", "V", "SC", "GP") and p.val(s) > 0]

    SEMS = {}
    sem_guards = []
    for s in SEMNAMES:
        gctx = nc.semaphore(s)
        SEMS[s] = gctx.__enter__()
        sem_guards.append(gctx)
    p.install(nc, SEMS, final_waits)
    for gctx in reversed(sem_guards):
        gctx.__exit__(None, None, None)
    return nc


_NC_CACHE = None


def kernel(L0_attn_x, L0_attn_w_qkv, L0_attn_w_proj, L0_attn_b_proj,
           L0_ffn_w_gate, L0_ffn_w_up, L0_ffn_w_down):
    global _NC_CACHE
    if _NC_CACHE is None:
        _NC_CACHE = build_program()
    nc = _NC_CACHE

    x = np.ascontiguousarray(np.asarray(L0_attn_x, dtype=np.float32))
    wqkv = np.asarray(L0_attn_w_qkv, dtype=np.float32)
    wqkv = np.ascontiguousarray(
        wqkv.reshape(8, 128, 24, 128).transpose(2, 1, 0, 3).reshape(3072, 1024))
    wproj = np.ascontiguousarray(np.asarray(L0_attn_w_proj, dtype=np.float32))
    bproj = np.ascontiguousarray(np.asarray(L0_attn_b_proj, dtype=np.float32)).reshape(1, D)
    wgT = np.ascontiguousarray(np.asarray(L0_ffn_w_gate, dtype=np.float32).T)
    wuT = np.ascontiguousarray(np.asarray(L0_ffn_w_up, dtype=np.float32).T)
    wdT = np.ascontiguousarray(np.asarray(L0_ffn_w_down, dtype=np.float32).T)

    iv64 = 1.0 / (10000.0 ** (np.arange(0, 64, 2, dtype=np.float64) / 64.0))
    invf_hi = iv64.astype(np.float32).reshape(32, 1)
    invf_lo = (iv64 - invf_hi.astype(np.float64).reshape(-1)).astype(np.float32).reshape(32, 1)

    shared = {
        "wqkv": wqkv, "wproj": wproj, "bproj": bproj,
        "wgT": wgT, "wuT": wuT, "wdT": wdT,
        "invf_hi": invf_hi, "invf_lo": invf_lo,
    }
    in_maps = [dict(shared, x=np.ascontiguousarray(x[b])) for b in range(8)]
    res = run_bass_kernel_spmd(nc, in_maps, core_ids=list(range(8)))
    out = np.stack([np.asarray(res.results[b]["out"]) for b in range(8)], axis=0)
    return out.astype(np.float32)


# version 20
# speedup vs baseline: 1.0747x; 1.0261x over previous
import numpy as np
import ml_dtypes
import concourse.bass as bass
import concourse.mybir as mybir
from concourse.bass_utils import run_bass_kernel_spmd

F32 = mybir.dt.float32
BF16 = mybir.dt.bfloat16
I32 = mybir.dt.int32
AF = mybir.ActivationFunctionType
ALU = mybir.AluOpType
AX = mybir.AxisListType.X

S = 2048
D = 1024
INNER = 4096
NT = S // 128
NCH = D // 128
NB = S // 256
PI = float(np.float32(np.pi))
TWO_PI = float(np.float32(2.0 * np.pi))
HALF_PI = float(np.float32(np.pi / 2.0))

ENGINES = ("sync", "gpsimd", "scalar", "vector", "tensor")
SEMNAMES = ("dS", "dG", "SA0", "SA1", "TG0", "TG1", "TD0", "TD1", "XR", "XQ",
            "GA0", "GA1", "GO", "RT0", "RT1", "RT2", "RT3", "T", "V", "SC", "GP",
            "SW0", "SW1", "SW2", "SW3", "GW0", "GW1")
DTSIZE = {F32: 4, BF16: 2, I32: 4}


class Prog:
    def __init__(self):
        self.streams = {e: [] for e in ENGINES}
        self.tot = {s: 0 for s in SEMNAMES}

    def val(self, sem):
        return self.tot[sem]

    def emit(self, eng, build, sem=None, n=1, waits=()):
        wv = []
        for w in waits:
            if isinstance(w, tuple):
                s, v = w
                wv.append((s, self.tot[s] if v is None else v))
            else:
                wv.append((w, self.tot[w]))
        if sem is not None:
            self.tot[sem] += n

        def fn(e, SEMS, waited):
            for s, v in wv:
                if v > 0 and waited.get(s, 0) < v:
                    e.wait_ge(SEMS[s], v)
                    waited[s] = v
            ins = build(e)
            if sem is not None and ins is not None:
                ins.then_inc(SEMS[sem], n)

        self.streams[eng].append(fn)

    def barrier(self):
        snap = dict(self.tot)
        owner = {"dS": "sync", "dG": "gpsimd", "GP": "gpsimd",
                 "SA0": "sync", "SA1": "sync", "TG0": "sync", "TG1": "sync",
                 "TD0": "sync", "TD1": "sync", "XR": "sync", "XQ": "sync",
                 "GA0": "gpsimd", "GA1": "gpsimd", "GO": "gpsimd",
                 "RT0": "gpsimd", "RT1": "gpsimd", "RT2": "gpsimd", "RT3": "gpsimd",
                 "SW0": "sync", "SW1": "sync", "SW2": "sync", "SW3": "sync",
                 "GW0": "gpsimd", "GW1": "gpsimd",
                 "T": "tensor", "V": "vector", "SC": "scalar"}
        for eng in ENGINES:
            waits = [(s, snap[s]) for s in SEMNAMES if owner[s] != eng and snap[s] > 0]
            self.emit(eng, lambda e: None, waits=waits)

    def install(self, nc, SEMS, final_waits):
        with nc.Block() as blk:
            for eng in ENGINES:
                fns = list(self.streams[eng])

                def make(fns, eng):
                    def body(e):
                        waited = {}
                        for fn in fns:
                            fn(e, SEMS, waited)
                        if eng == "sync":
                            for s, v in final_waits:
                                e.wait_ge(SEMS[s], v)
                    return body

                getattr(blk, eng)(make(fns, eng))


def build_program(phases="TWABC"):
    import os
    A1N = int(os.environ.get("A1N", str(NT)))
    A1SKIP = set(os.environ.get("A1SKIP", "").split(","))
    CN = int(os.environ.get("CN", "4"))
    CSKIP = set(os.environ.get("CSKIP", "").split(","))
    nc = bass.Bass("TRN2", target_bir_lowering=False)
    p = Prog()

    def emit(eng, b, **kw):
        p.emit(eng, b, **kw)

    x_d = nc.dram_tensor("x", [S, D], F32, kind="ExternalInput")
    wqkv_d = nc.dram_tensor("wqkv", [3 * D, D], F32, kind="ExternalInput")
    wproj_d = nc.dram_tensor("wproj", [D, D], F32, kind="ExternalInput")
    bproj_d = nc.dram_tensor("bproj", [1, D], F32, kind="ExternalInput")
    wgT_d = nc.dram_tensor("wgT", [D, INNER], F32, kind="ExternalInput")
    wuT_d = nc.dram_tensor("wuT", [D, INNER], F32, kind="ExternalInput")
    wdT_d = nc.dram_tensor("wdT", [INNER, D], F32, kind="ExternalInput")
    invf_hi_d = nc.dram_tensor("invf_hi", [32, 1], F32, kind="ExternalInput")
    invf_lo_d = nc.dram_tensor("invf_lo", [32, 1], F32, kind="ExternalInput")
    out_d = nc.dram_tensor("out", [S, D], F32, kind="ExternalOutput")

    SCRK = "ExternalOutput" if os.environ.get("DBG_SCRATCH") else "Internal"
    DBGA1 = bool(os.environ.get("DBG_A1"))
    if DBGA1:
        dbgst_d = nc.dram_tensor("dbgst", [NT * 128, 16], F32, kind="ExternalOutput")
        dbgxn_d = nc.dram_tensor("dbgxn", [NT * 128, D], F32, kind="ExternalOutput")
    DBGC = bool(os.environ.get("DBG_C"))
    if DBGC:
        dbghq_d = nc.dram_tensor("dbghq", [S, INNER], BF16, kind="ExternalOutput")
        dbghst_d = nc.dram_tensor("dbghst", [NT * 128, 48], F32, kind="ExternalOutput")
        dbghqf_d = nc.dram_tensor("dbghqf", [NT * 128, 16384], BF16, kind="ExternalOutput")
    DBGA2 = bool(os.environ.get("DBG_A2"))
    if DBGA2:
        dbgq_d = nc.dram_tensor("dbgq", [128, NCH * S], F32, kind="ExternalOutput")
        dbgk_d = nc.dram_tensor("dbgk", [128, NCH * S], F32, kind="ExternalOutput")
        dbgv_d = nc.dram_tensor("dbgv", [128, NT * D], F32, kind="ExternalOutput")
        dbgo_d = nc.dram_tensor("dbgo", [NB * 128, 8 * 256], F32, kind="ExternalOutput")
    terng_d = nc.dram_tensor("terng_scr", [D, INNER], BF16, kind=SCRK)
    ternu_d = nc.dram_tensor("ternu_scr", [D, INNER], BF16, kind=SCRK)
    ternd_d = nc.dram_tensor("ternd_scr", [INNER, D], BF16, kind=SCRK)
    xqfm_d = nc.dram_tensor("xqfm_scr", [D, S], BF16, kind=SCRK)
    attn_d = nc.dram_tensor("attn_scr", [S, D], BF16, kind=SCRK)

    BASE = 24576
    cursor = [BASE]

    OFFS = {}

    def alloc(name, shape, dtype, at=None):
        nbytes = int(np.prod(shape[1:])) * DTSIZE[dtype]
        off = cursor[0] if at is None else at
        assert off % 32 == 0, (name, off)
        t = nc.alloc_sbuf_tensor_at(name, list(shape), dtype, offset=off)
        OFFS[id(t)] = off
        if at is None:
            cursor[0] = off + ((nbytes + 31) // 32) * 32
        return t

    def offof(t):
        return OFFS[id(t)]

    consts = alloc("consts", [128, 8], F32)
    onesf = alloc("onesf", [128, 1], F32)
    ones_bf = alloc("ones_bf", [128, 64], BF16)
    onesrow_f = alloc("onesrow_f", [1, 128], F32)
    onesrow_bf = alloc("onesrow_bf", [1, 128], BF16)
    ident = alloc("ident", [128, 128], BF16)
    Prot = alloc("Prot", [128, 128], BF16)
    maskAB = alloc("maskAB", [128, 512], F32)
    cosF = alloc("cosF", [128, S], BF16)
    sinS = alloc("sinS", [128, S], BF16)
    cx = alloc("cx", [128, NT], F32)
    crep = alloc("crep", [128, 8], F32)
    stage = alloc("stage", [1, 8], F32)
    bproj_bf = alloc("bproj_bf", [1, D], BF16)
    stats = alloc("stats", [128, 16], F32)
    hstats = alloc("hstats", [128, 56], F32)
    PERS_END = cursor[0]

    cursor[0] = PERS_END
    xnfm = alloc("xnfm", [128, NCH * S], BF16)
    kfm = alloc("kfm", [128, NCH * S], BF16)
    qfm = alloc("qfm", [128, NCH * S], BF16)
    vtm = alloc("vtm", [128, NT * D], BF16)
    wprojb = alloc("wprojb", [128, NCH * D], BF16, at=PERS_END)
    xbuf = [alloc("xbuf0", [128, D], F32), alloc("xbuf1", [128, D], F32)]
    xntm = alloc("xntm", [128, D], F32)
    xqi = alloc("xqi", [128, D], I32)
    xqb = [alloc("xqb0", [128, D], BF16), alloc("xqb1", [128, D], BF16)]
    xnb = [alloc("xnb0", [128, D], BF16), alloc("xnb1", [128, D], BF16)]
    xqstg = [alloc("xqstg0", [128, D], BF16), alloc("xqstg1", [128, D], BF16)]
    ast = alloc("ast", [128, 240], F32, at=offof(xntm))
    wqb = [alloc("wqb0", [128, 1024], F32), alloc("wqb1", [128, 1024], F32)]
    wqbf = [alloc("wqbf0", [128, 1024], BF16), alloc("wqbf1", [128, 1024], BF16)]
    qsb = [alloc("qsb%d" % i, [128, 512], BF16) for i in range(4)]
    tP = alloc("tP", [128, 512], BF16)
    tR = alloc("tR", [128, 512], BF16)
    AB_END = cursor[0]
    utab = alloc("utab", [128, S], F32, at=offof(kfm))
    utmp2 = alloc("utmp2", [128, S], F32, at=offof(kfm) + 8192)
    ts1 = alloc("ts1", [128, S], F32, at=offof(kfm) + 16384)
    ts2 = alloc("ts2", [128, S], F32, at=offof(kfm) + 24576)
    utmp = alloc("utmp", [128, 1024], F32, at=offof(qfm))
    bpf = alloc("bpf", [1, D], F32, at=offof(qfm) + 4096)
    fbuf = alloc("fbuf", [128, 1], F32, at=offof(qfm) + 8192)
    flbuf = alloc("flbuf", [128, 1], F32, at=offof(qfm) + 8224)
    wbuf4 = [alloc("wb4_%d" % i, [128, 1024], F32, at=offof(vtm) + 4096 * i) for i in range(4)]
    wsA = [alloc("wsA0", [128, 1024], F32, at=offof(vtm) + 16384),
           alloc("wsA1", [128, 1024], F32, at=offof(vtm) + 20480)]
    wsB = [alloc("wsB0", [128, 1024], F32, at=offof(vtm) + 24576),
           alloc("wsB1", [128, 1024], F32, at=offof(vtm) + 28672)]
    wternb = [alloc("wternb0", [128, 1024], BF16), alloc("wternb1", [128, 1024], BF16)]
    wsum = alloc("wsum", [128, 96], F32)
    acc2 = alloc("acc2", [128, 1], F32)
    xsq = alloc("xsq", [128, 1024], F32, at=offof(xqi))
    vtmp = [alloc("vtmp0", [128, 512], BF16, at=offof(xqstg[0])),
            alloc("vtmp1", [128, 512], BF16, at=offof(xqstg[1]))]
    Eb = [alloc("Eb0", [128, 1024], BF16, at=offof(xbuf[0])),
          alloc("Eb1", [128, 1024], BF16, at=offof(xbuf[1]))]
    Ofm = alloc("Ofm", [128, 8 * 256], BF16, at=offof(xntm))
    lnl = alloc("lnl", [128, 256], F32, at=offof(xqi))
    recl = alloc("recl", [128, 256], F32, at=offof(xqi) + 1024)
    attst = [alloc("attst0", [128, 512], BF16, at=offof(xqb[0])),
             alloc("attst1", [128, 512], BF16, at=offof(xnb[0]))]

    cursor[0] = PERS_END
    hB = alloc("hB", [128, 4 * 4096], F32)
    hqfm = alloc("hqfm", [128, 16384], BF16)
    terng = [alloc("terngA", [128, 4096], BF16), alloc("terngB", [128, 4096], BF16)]
    ternu = [alloc("ternuA", [128, 4096], BF16), alloc("ternuB", [128, 4096], BF16)]
    terndP = [alloc("terndA", [128, 2048], BF16), alloc("terndB", [128, 2048], BF16)]
    xqf = alloc("xqf", [128, 4096], BF16)
    hqbf = alloc("hqbf", [128, 4096], BF16)
    hqi = [alloc("hqi0", [128, 512], I32)]
    hqi.append(hqi[0])
    gbuf = alloc("gbuf", [128, 512], F32)
    dbuf = alloc("dbuf", [128, 1024], F32)
    obuf_s = alloc("obuf", [128, 1024], F32)
    obuf = [obuf_s, obuf_s]
    obuf2_s = alloc("obuf2", [128, 1024], F32)
    obuf2 = [obuf2_s, obuf2_s]
    xrel_s = alloc("xrel", [128, 1024], F32)
    xrel = [xrel_s, xrel_s]
    attb_s = alloc("attb", [128, 1024], BF16)
    attb = [attb_s, attb_s]
    C_END = cursor[0]
    assert max(AB_END, C_END) <= 229376, (PERS_END, AB_END, C_END)

    w_units = []
    w_pumped = [0]

    def pump_w(k, cap=None):
        lim = len(w_units) if cap is None else min(cap, len(w_units))
        while w_pumped[0] < lim and k > 0:
            w_units[w_pumped[0]]()
            w_pumped[0] += 1
            k -= 1

    wb_free = [None, None, None, None]
    wt_free = [None, None]
    w_ucount = [0]
    w_tcount = [0]

    def mk_passA(src_d_, col, kp, j):
        def unit():
            b = w_ucount[0] % 4
            w_ucount[0] += 1
            wws = [wb_free[b]] if wb_free[b] else []
            emit("sync", lambda e: e.dma_start(
                wbuf4[b][:], src_d_[128 * kp:128 * (kp + 1), 1024 * j:1024 * (j + 1)]),
                sem="SW%d" % b, n=16, waits=wws)
            emit("vector", lambda e: e.tensor_reduce(
                wsum[:, col:col + 1], wbuf4[b][:], AX, ALU.add,
                apply_absolute_value=True), sem="V", waits=["SW%d" % b])
            wb_free[b] = ("V", p.val("V"))
        return unit

    w_add_done = [0, 0]
    w_crep_v = [0]

    def mk_crep(ci, m):
        def unit():
            emit("vector", lambda e: e.tensor_reduce(
                acc2[:], wsum[:, 32 * m:32 * (m + 1)], AX, ALU.add), sem="V", waits=["V", "T"])
            emit("tensor", lambda e: e.matmul(
                pwq[0:1, ci:ci + 1], acc2[:], onesf[:], start=True, stop=True),
                sem="T", waits=["V"])
            emit("vector", lambda e: e.tensor_scalar(
                stage[0:1, 0:1], pwq[0:1, ci:ci + 1], 1.0 / (1024.0 * 4096.0), 1e-5,
                ALU.mult, ALU.max), sem="V", waits=["T", "V"])
            emit("vector", lambda e: e.tensor_scalar(stage[0:1, 2:3], stage[0:1, 0:1], 0.5, None, ALU.mult),
                 sem="V", waits=["V"])
            emit("vector", lambda e: e.reciprocal(stage[0:1, 3:4], stage[0:1, 0:1]), sem="V", waits=["V"])
            emit("tensor", lambda e: e.matmul(
                pwq[:, ci:ci + 2], onesrow_f[:], stage[0:1, 2:4], start=True, stop=True),
                sem="T", waits=["V"])
            emit("vector", lambda e: e.tensor_copy(crep[:, ci:ci + 2], pwq[:, ci:ci + 2]),
                 sem="V", waits=["T"])
            w_crep_v[0] = p.val("V")
        return unit

    def mk_passB(src_d_, dst_d_, ci, kp, j):
        def unit():
            b = w_ucount[0] % 4
            w_ucount[0] += 1
            tb_ = w_tcount[0] % 2
            w_tcount[0] += 1
            wws = [wb_free[b]] if wb_free[b] else []
            emit("sync", lambda e: e.dma_start(
                wbuf4[b][:], src_d_[128 * kp:128 * (kp + 1), 1024 * j:1024 * (j + 1)]),
                sem="SW%d" % b, n=16, waits=wws)
            emit("scalar", lambda e: e.activation(
                wsA[tb_][:], wbuf4[b][:], AF.Sign, bias=consts[:, 0:1], scale=crep[:, ci + 1:ci + 2]),
                waits=["SW%d" % b, ("V", max(w_add_done[tb_], w_crep_v[0]))])
            emit("scalar", lambda e: e.activation(
                wsB[tb_][:], wbuf4[b][:], AF.Sign, bias=consts[:, 1:2], scale=crep[:, ci + 1:ci + 2]),
                sem="SC")
            sc_after = p.val("SC")
            wb_free[b] = ("SC", sc_after)
            wws2 = [("SC", sc_after)] + ([wt_free[tb_]] if wt_free[tb_] else [])
            emit("vector", lambda e: e.tensor_add(wternb[tb_][:], wsA[tb_][:], wsB[tb_][:]),
                 sem="V", waits=wws2)
            w_add_done[tb_] = p.val("V")
            emit("gpsimd", lambda e: e.dma_start(
                dst_d_[128 * kp:128 * (kp + 1), 1024 * j:1024 * (j + 1)], wternb[tb_][:]),
                sem="GW%d" % tb_, n=16, waits=["V"])
            wt_free[tb_] = ("GW%d" % tb_, p.val("GW%d" % tb_))
        return unit

    if "W" in phases:
        for mi, (src_, dst_, ci_) in enumerate([(wgT_d, terng_d, 0), (wuT_d, ternu_d, 2)]):
            for kp in range(8):
                for j in range(4):
                    w_units.append(mk_passA(src_, 32 * mi + kp * 4 + j, kp, j))
        w_units.append(mk_crep(0, 0))
        w_units.append(mk_crep(2, 1))
        for kp in range(8):
            for j in range(4):
                w_units.append(mk_passB(wgT_d, terng_d, 0, kp, j))
        for kp in range(32):
            w_units.append(mk_passA(wdT_d, 64 + kp, kp, 0))
        w_units.append(mk_crep(4, 2))
        for kp in range(8):
            for j in range(4):
                w_units.append(mk_passB(wuT_d, ternu_d, 2, kp, j))
        for kp in range(32):
            w_units.append(mk_passB(wdT_d, ternd_d, 4, kp, 0))

    for b in (
        lambda e: e.memset(consts[:, 0:1], -0.5),
        lambda e: e.memset(consts[:, 1:2], 0.5),
        lambda e: e.memset(consts[:, 2:3], -20.0),
        lambda e: e.memset(consts[0:32, 3:4], -1.0),
        lambda e: e.memset(consts[32:64, 3:4], 1.0),
        lambda e: e.memset(consts[64:96, 3:4], -1.0),
        lambda e: e.memset(consts[96:128, 3:4], 1.0),
        lambda e: e.memset(onesf[:], 1.0),
        lambda e: e.memset(ones_bf[:], 1.0),
        lambda e: e.memset(onesrow_f[:], 1.0),
        lambda e: e.memset(onesrow_bf[:], 1.0),
        lambda e: e.memset(ident[:], 0.0),
        lambda e: e.affine_select(out=ident[:], in_=ident[:], compare_op=ALU.not_equal,
                                  fill=1.0, base=0, pattern=[[-1, 128]], channel_multiplier=1),
        lambda e: e.memset(maskAB[:], 0.0),
        lambda e: e.affine_select(out=maskAB[:, 0:256], in_=maskAB[:, 0:256], compare_op=ALU.is_ge,
                                  fill=-1e9, base=0, pattern=[[1, 256]], channel_multiplier=-1),
    ):
        emit("gpsimd", b)
    emit("gpsimd", lambda e: e.affine_select(out=maskAB[:, 256:512], in_=maskAB[:, 256:512],
                                             compare_op=ALU.is_ge, fill=-1e9, base=-128,
                                             pattern=[[1, 256]], channel_multiplier=-1), sem="GP")
    for (dlo, slo) in ((0, 32), (32, 0), (64, 96), (96, 64)):
        emit("gpsimd", (lambda dlo, slo: lambda e: e.dma_start(
            Prot[dlo:dlo + 32, :], ident[slo:slo + 32, :]))(dlo, slo), sem="dG", n=16)
    prot_dg = p.val("dG")

    for off in (0, 32, 64, 96):
        emit("sync", (lambda off: lambda e: e.dma_start(fbuf[off:off + 32, :], invf_hi_d[:]))(off),
             sem="dS", n=16)
        emit("sync", (lambda off: lambda e: e.dma_start(flbuf[off:off + 32, :], invf_lo_d[:]))(off),
             sem="dS", n=16)
    emit("sync", lambda e: e.dma_start(bpf[0:1, :], bproj_d[:]), sem="dS", n=16)
    emit("vector", lambda e: e.tensor_copy(bproj_bf[0:1, :], bpf[0:1, :]), waits=["dS"])
    def emit_wrap(dst_fn, src_fn, n, ta, tb, sem=None):
        emit("vector", lambda e: e.tensor_scalar(ta[:, 0:n], src_fn(), -PI, None, ALU.is_lt),
             sem="V", waits=["V"])
        emit("vector", lambda e: e.tensor_scalar(tb[:, 0:n], src_fn(), PI, None, ALU.is_gt),
             sem="V", waits=["V"])
        emit("vector", lambda e: e.tensor_sub(ta[:, 0:n], ta[:, 0:n], tb[:, 0:n]),
             sem="V", waits=["V"])
        emit("vector", lambda e: e.scalar_tensor_tensor(
            dst_fn(), ta[:, 0:n], TWO_PI, src_fn(), ALU.mult, ALU.add), sem=(sem or "V"), waits=["V"])

    emit("vector", lambda e: e.memset(utab[:, 0:1], 0.0), sem="V")
    emit("vector", lambda e: e.tensor_add(utab[:, 1:2], fbuf[:], flbuf[:]), sem="V", waits=["V"])
    sz = 1
    while sz < S:
        if sz >= 2:
            hs_ = sz // 2
            emit("vector", (lambda hs_: lambda e: e.tensor_scalar(
                utmp[:, 0:1], utab[:, hs_:hs_ + 1], utab[:, hs_:hs_ + 1], None, ALU.add))(hs_),
                sem="V", waits=["V"])
            emit_wrap((lambda sz: lambda: utab[:, sz:sz + 1])(sz), lambda: utmp[:, 0:1],
                      1, ts1, ts2)
        n = min(sz, S - sz)
        emit("vector", (lambda sz, n: lambda e: e.tensor_scalar(
            utmp[:, 0:n], utab[:, 0:n], utab[:, sz:sz + 1], None, ALU.add))(sz, n),
            sem="V", waits=["V"])
        emit_wrap((lambda sz, n: lambda: utab[:, sz:sz + n])(sz, n),
                  (lambda n: lambda: utmp[:, 0:n])(n), n, ts1, ts2, sem="V")
        pump_w(1, cap=64)
        sz *= 2
    emit("scalar", lambda e: e.activation(ts1[:, 0:S], utab[:], AF.Sin, bias=0.0, scale=1.0),
         sem="SC", waits=["V"])
    emit("vector", lambda e: e.tensor_scalar(sinS[:], ts1[:, 0:S], consts[:, 3:4], None, ALU.mult),
         waits=["SC", "GP"])
    emit("vector", lambda e: e.tensor_scalar(ts2[:, 0:S], utab[:], HALF_PI, None, ALU.add), sem="V")
    emit_wrap(lambda: utmp2[:, 0:S], lambda: ts2[:, 0:S], S, ts1, utab, sem="V")
    emit("scalar", lambda e: e.activation(cosF[:], utmp2[:, 0:S], AF.Sin, bias=0.0, scale=1.0),
         sem="SC", waits=["V"])
    pump_w(3, cap=64)

    t_tbl_sc = p.val("SC")

    if True:
      with (
        nc.psum_tensor("pqkv0", [128, 512], F32) as pq0,
        nc.psum_tensor("pqkv1", [128, 512], F32) as pq1,
        nc.psum_tensor("ptr0", [128, 1024], BF16) as pt0,
        nc.psum_tensor("ptr1", [128, 1024], BF16) as pt1,
        nc.psum_tensor("pwq", [128, 8], F32) as pwq,
        nc.psum_tensor("prot0", [128, 512], F32) as pr0,
        nc.psum_tensor("prot1", [128, 512], F32) as pr1,
      ):
        pqkv = [pq0, pq1]
        ptr = [pt0, pt1]
        xb_free = [[], []]
        ptr_free = [("V", 0), ("V", 0)]
        xqs_free = [("GA0", 0), ("GA1", 0)]
        xnfm_r = xnfm[:].rearrange("p (c tb w) -> p c tb w", c=NCH, tb=NT)
        xq_dst = xqfm_d[:].rearrange("(c p) (tb w) -> p c tb w", p=128, w=128)
        A1R = range(A1N) if ("A" in phases or "1" in phases) else []

        def dma_x(t, par):
            emit("sync", (lambda t, par: lambda e: e.dma_start(
                xbuf[par][:], x_d[128 * t:128 * (t + 1), :]))(t, par),
                sem="SA%d" % par, n=16, waits=list(xb_free[par]))
        for t in A1R:
            par = t % 2
            dma_x(t, par)
            emit("vector", (lambda t, par: lambda e: e.reduce_sum(
                ast[:, t:t + 1], xbuf[par][:], AX))(t, par), sem="V", waits=["SA%d" % par])
            emit("scalar", (lambda t, par: lambda e: e.activation(
                xsq[:], xbuf[par][:], AF.Square, bias=0.0, scale=1.0,
                accum_out=ast[:, 16 + t:17 + t]))(t, par), sem="SC", waits=["SA%d" % par])
            xb_free[par] = [("V", p.val("V")), ("SC", p.val("SC"))]
            pump_w(2)
        if A1N:
            emit("vector", lambda e: e.tensor_scalar(ast[:, 32:48], ast[:, 0:16], 1.0 / D, None, ALU.mult),
                 sem="V", waits=["V", "SC"])
            emit("vector", lambda e: e.tensor_scalar(ast[:, 48:64], ast[:, 16:32], 1.0 / D, 1e-5, ALU.mult, ALU.add),
                 sem="V", waits=["V"])
            emit("vector", lambda e: e.tensor_mul(ast[:, 64:80], ast[:, 32:48], ast[:, 32:48]),
                 sem="V", waits=["V"])
            emit("vector", lambda e: e.tensor_sub(ast[:, 80:96], ast[:, 48:64], ast[:, 64:80]),
                 sem="V", waits=["V"])
            emit("scalar", lambda e: e.sqrt(ast[:, 96:112], ast[:, 80:96]), sem="SC", waits=["V"])
            emit("vector", lambda e: e.reciprocal(ast[:, 112:128], ast[:, 96:112]), sem="V", waits=["SC", "V"])
            emit("vector", lambda e: e.tensor_mul(ast[:, 128:144], ast[:, 32:48], ast[:, 112:128]),
                 sem="V", waits=["V"])
            pump_w(2)
        for t in A1R:
            par = t % 2
            dma_x(t, par)
            emit("vector", (lambda t, par: lambda e: e.tensor_scalar(
                xsq[:], xbuf[par][:], ast[:, 112 + t:113 + t], ast[:, 128 + t:129 + t],
                ALU.mult, ALU.subtract))(t, par), sem="V", waits=["SA%d" % par])
            emit("vector", (lambda t: lambda e: e.tensor_reduce(
                ast[:, 144 + t:145 + t], xsq[:], AX, ALU.max,
                apply_absolute_value=True))(t), sem="V", waits=["V"])
            xb_free[par] = [("V", p.val("V"))]
            pump_w(3)
        if A1N:
            emit("vector", lambda e: e.tensor_scalar(ast[:, 160:176], ast[:, 144:160], 1e-5, None, ALU.max),
                 sem="V", waits=["V"])
            emit("vector", lambda e: e.tensor_scalar(cx[:, 0:NT], ast[:, 160:176], 1.0 / 127.0, None, ALU.mult),
                 sem="V", waits=["V"])
            emit("vector", lambda e: e.reciprocal(ast[:, 176:192], ast[:, 160:176]), sem="V", waits=["V"])
            emit("vector", lambda e: e.tensor_scalar(ast[:, 192:208], ast[:, 176:192], 127.0, None, ALU.mult),
                 sem="V", waits=["V"])
            emit("vector", lambda e: e.tensor_mul(ast[:, 208:224], ast[:, 112:128], ast[:, 192:208]),
                 sem="V", waits=["V"])
            emit("vector", lambda e: e.tensor_mul(ast[:, 224:240], ast[:, 128:144], ast[:, 192:208]),
                 sem="V", waits=["V"])
            pump_w(2)
        for t in A1R:
            par = t % 2
            dma_x(t, par)
            emit("vector", (lambda t, par: lambda e: e.tensor_scalar(
                xnb[par][:], xbuf[par][:], ast[:, 112 + t:113 + t], ast[:, 128 + t:129 + t],
                ALU.mult, ALU.subtract))(t, par), sem="V", waits=["SA%d" % par])
            for c in range(NCH):
                emit("tensor", (lambda c, par: lambda e: e.transpose(
                    ptr[par][:, 128 * c:128 * (c + 1)], xnb[par][:, 128 * c:128 * (c + 1)], ident[:]))(c, par),
                    waits=(["V", ptr_free[par]] if c == 0 else []),
                    sem=("T" if c == NCH - 1 else None))
            emit("vector", (lambda t, par: lambda e: e.tensor_scalar(
                xqi[:], xbuf[par][:], ast[:, 208 + t:209 + t], ast[:, 224 + t:225 + t],
                ALU.mult, ALU.subtract))(t, par))
            emit("vector", (lambda par: lambda e: e.tensor_scalar(
                xqb[par][:], xqi[:], 1.0, None, ALU.mult))(par), sem="V")
            xb_free[par] = [("V", p.val("V"))]
            emit("vector", (lambda t, par: lambda e: e.tensor_copy(
                xnfm_r[:, 0:4, t, :], ptr[par][:, 0:512].rearrange("p (c w) -> p c w", c=4)))(t, par),
                waits=["T"])
            emit("vector", (lambda t, par: lambda e: e.tensor_copy(
                xnfm_r[:, 4:8, t, :], ptr[par][:, 512:1024].rearrange("p (c w) -> p c w", c=4)))(t, par),
                sem="V")
            for c in range(NCH):
                emit("tensor", (lambda c, par: lambda e: e.transpose(
                    ptr[par][:, 128 * c:128 * (c + 1)], xqb[par][:, 128 * c:128 * (c + 1)], ident[:]))(c, par),
                    waits=(["V"] if c == 0 else []),
                    sem=("T" if c == NCH - 1 else None))
            if "xq" not in A1SKIP:
                emit("scalar", (lambda par: lambda e: e.activation(
                    xqstg[par][:], ptr[par][:], AF.Copy, bias=0.0, scale=1.0))(par),
                    sem="SC", waits=["T", xqs_free[par]])
                ptr_free[par] = ("SC", p.val("SC"))
                emit("gpsimd", (lambda t, par: lambda e: e.dma_start(
                    xq_dst[:, :, t, :], xqstg[par][:].rearrange("p (c w) -> p c w", c=NCH)))(t, par),
                    sem="GA%d" % par, n=16, waits=["SC"])
                xqs_free[par] = ("GA%d" % par, p.val("GA%d" % par))
            else:
                ptr_free[par] = ("T", p.val("T"))
            pump_w(3)

        a1_dg = p.val("dG")
        wq_free = [0, 0]
        pq_free = [0, 0]
        qs_free = [0, 0]
        vt_free = [0, 0]
        ptr_free = [("SC", p.val("SC")), ("SC", p.val("SC"))]
        prot_ps = [pr0, pr1]
        rotps_free = [0, 0]
        rp_ctr = [0]
        s1q = []
        s2q = []

        def emit_rot(ent):
            (qpar, tb2, dest2, cch2, cast_v) = ent
            rpar = rp_ctr[0] % 2
            rp_ctr[0] += 1
            emit("tensor", (lambda qpar, rpar: lambda e: e.matmul(
                prot_ps[rpar][:], Prot[:], qsb[qpar][:], start=True, stop=True))(qpar, rpar),
                waits=[("V", cast_v), ("V", rotps_free[rpar]), ("dG", prot_dg)], sem="T")
            s2q.append((qpar, tb2, dest2, cch2, rpar, p.val("T")))

        def emit_ropeV(ent):
            (qpar, tb2, dest2, cch2, rpar, rtv) = ent
            emit("vector", (lambda qpar, tb2: lambda e: e.tensor_mul(
                tP[:], qsb[qpar][:], cosF[:, 512 * tb2:512 * (tb2 + 1)]))(qpar, tb2),
                waits=[("SC", t_tbl_sc)])
            emit("vector", (lambda rpar, tb2: lambda e: e.tensor_mul(
                tR[:], prot_ps[rpar][:], sinS[:, 512 * tb2:512 * (tb2 + 1)]))(rpar, tb2),
                waits=[("T", rtv)])
            emit("vector", (lambda dest2, cch2, tb2: lambda e: e.tensor_add(
                dest2[:, S * cch2 + 512 * tb2: S * cch2 + 512 * (tb2 + 1)], tP[:], tR[:]))(dest2, cch2, tb2),
                sem="V")
            rotps_free[rpar] = p.val("V")

        def rope_push(ent):
            s1q.append(ent)
            if len(s1q) >= 2:
                emit_rot(s1q.pop(0))
            if len(s2q) >= 2:
                emit_ropeV(s2q.pop(0))

        def rope_flush():
            while s1q:
                emit_rot(s1q.pop(0))
            while s2q:
                emit_ropeV(s2q.pop(0))
        vdst = vtm[:].rearrange("p (tk col) -> p tk col", col=D)
        w_flush_waits = []
        for dc in (range(24) if ("A" in phases or "2" in phases) else []):
            if dc == 16:
                rope_flush()
                pump_w(1 << 30)
                w_flush_waits = [("SC", p.val("SC")), ("GW0", p.val("GW0")),
                                 ("GW1", p.val("GW1"))]
            else:
                pump_w(4)
            par = dc % 2
            emit("sync", (lambda dc, par: lambda e: e.dma_start(
                wqb[par][:], wqkv_d[128 * dc:128 * (dc + 1), :]))(dc, par),
                sem="SA%d" % par, n=16, waits=[("T", wq_free[par])])
            emit("vector", (lambda par: lambda e: e.tensor_copy(wqbf[par][:], wqb[par][:]))(par),
                 sem="V", waits=["SA%d" % par])
            cast_done = p.val("V")
            for tb in range(4):
                ppar = (dc * 4 + tb) % 2
                for di in range(NCH):
                    emit("tensor", (lambda par, ppar, di, tb: lambda e: e.matmul(
                        pqkv[ppar][:], wqbf[par][:, 128 * di:128 * (di + 1)],
                        xnfm[:, S * di + 512 * tb: S * di + 512 * (tb + 1)],
                        start=(di == 0), stop=(di == NCH - 1)))(par, ppar, di, tb),
                        waits=([("V", cast_done), ("V", pq_free[ppar])] if di == 0 else []),
                        sem=("T" if di == NCH - 1 else None))
                if dc < 16:
                    qpar = (dc * 4 + tb) % 4
                    dest = qfm if dc < 8 else kfm
                    cch = dc % 8
                    emit("vector", (lambda ppar, qpar: lambda e: e.tensor_copy(
                        qsb[qpar][:], pqkv[ppar][:]))(ppar, qpar),
                        sem="V", waits=["T"])
                    pq_free[ppar] = p.val("V")
                    rope_push((qpar, tb, dest, cch, p.val("V")))
                else:
                    vch = dc - 16
                    vpar = (dc * 4 + tb) % 2
                    emit("vector", (lambda ppar, vpar: lambda e: e.tensor_copy(
                        vtmp[vpar][:], pqkv[ppar][:]))(ppar, vpar), sem="V",
                        waits=["T", ("T", vt_free[vpar]), "GA0", "GA1"])
                    pq_free[ppar] = p.val("V")
                    for s_ in range(4):
                        emit("tensor", (lambda vpar, s_: lambda e: e.transpose(
                            ptr[vpar][:, 128 * s_:128 * (s_ + 1)], vtmp[vpar][:, 128 * s_:128 * (s_ + 1)],
                            ident[:]))(vpar, s_),
                            waits=(["V", ptr_free[vpar]] if s_ == 0 else []),
                            sem=("T" if s_ == 3 else None))
                    vt_free[vpar] = p.val("T")
                    emit("vector", (lambda vpar, vch, tb: lambda e: e.tensor_copy(
                        vdst[:, 4 * tb:4 * (tb + 1), 128 * vch:128 * (vch + 1)],
                        ptr[vpar][:, 0:512].rearrange("p (s w) -> p s w", s=4)))(vpar, vch, tb),
                        sem="V", waits=["T"] + w_flush_waits)
                    w_flush_waits = []
                    ptr_free[vpar] = ("V", p.val("V"))
            wq_free[par] = p.val("T")

        if DBGA2:
            emit("gpsimd", lambda e: e.dma_start(dbgq_d[:], qfm[:]), sem="dG", n=16,
                 waits=["V", ("T", p.val("T"))])
            emit("gpsimd", lambda e: e.dma_start(dbgk_d[:], kfm[:]), sem="dG", n=16)
            emit("gpsimd", lambda e: e.dma_start(dbgv_d[:], vtm[:]), sem="dG", n=16)
        rope_flush()
        wp_t = p.val("T")
        for c in (range(NCH) if ("A" in phases or "2" in phases) else []):
            emit("sync", (lambda c, wp_t: lambda e: e.dma_start(
                wqb[0][:], wproj_d[128 * c:128 * (c + 1), :]))(c, wp_t), sem="SA0", n=16,
                waits=[("T", wp_t), ("V", p.val("V"))])
            emit("vector", (lambda c: lambda e: e.tensor_copy(
                wprojb[:, D * c:D * (c + 1)], wqb[0][:]))(c), sem="V", waits=["SA0"])
        pump_w(1 << 30)

    if "B" in phases:
      with (
        nc.psum_tensor("pSh0", [128, 1024], F32) as pSh0,
        nc.psum_tensor("pSh1", [128, 1024], F32) as pSh1,
        nc.psum_tensor("pO", [128, 256], F32) as pO,
        nc.psum_tensor("pl", [128, 256], F32) as pl,
        nc.psum_tensor("pproj0", [128, 512], F32) as pp0,
        nc.psum_tensor("pproj1", [128, 512], F32) as pp1,
      ):
        pproj = [pp0, pp1]
        att_free = [0, 0]
        sc_bank_done = [0, 0]
        for b in range(NB):
            ntk = 2 * (b + 1)
            for cch in range(NCH):
                v_norm_done = p.val("V")
                nsteps = ntk // 2

                def emit_qk(jj):
                    bank = jj % 2
                    for j in (2 * jj, 2 * jj + 1):
                        qtr = j % 4
                        for (hs, psum, tpos) in ((0, pSh0, (0, 0)), (64, pSh1, (64, 0))):
                            w = [("SC", sc_bank_done[bank])] if j == 2 * jj and hs == 0 else []
                            emit("tensor", (lambda j, qtr, hs, psum, tpos, cch, b: lambda e: e.matmul(
                                psum[:, 256 * qtr:256 * (qtr + 1)],
                                kfm[hs:hs + 64, S * cch + 128 * j: S * cch + 128 * (j + 1)],
                                qfm[hs:hs + 64, S * cch + 256 * b: S * cch + 256 * (b + 1)],
                                start=True, stop=True,
                                tile_position=tpos))(j, qtr, hs, psum, tpos, cch, b),
                                waits=w, sem=("T" if hs == 64 else None))

                def emit_exp(jj):
                    j1 = 2 * jj + 1
                    qtr0 = (2 * jj) % 4
                    bank = jj % 2
                    tw = [("T", qk_done[jj]), ("T", avl_done.get(jj - 2, 0))]
                    if j1 == ntk - 1:
                        for psum in (pSh0, pSh1):
                            emit("vector", (lambda psum, qtr0: lambda e: e.tensor_tensor(
                                psum[:, 256 * qtr0:256 * qtr0 + 512],
                                psum[:, 256 * qtr0:256 * qtr0 + 512], maskAB[:], ALU.add))(psum, qtr0),
                                sem="V", waits=tw)
                    for (psum, ebuf) in ((pSh0, Eb[0]), (pSh1, Eb[1])):
                        emit("scalar", (lambda psum, ebuf, qtr0, bank: lambda e: e.activation(
                            ebuf[:, 512 * bank:512 * (bank + 1)], psum[:, 256 * qtr0:256 * qtr0 + 512],
                            AF.Exp, bias=consts[:, 2:3], scale=0.125))(psum, ebuf, qtr0, bank),
                            sem="SC", waits=tw + ["V"])
                    sc_bank_done[bank] = p.val("SC")
                    exp_done[jj] = p.val("SC")

                def emit_avl(jj):
                    bank = jj % 2
                    for j in (2 * jj, 2 * jj + 1):
                        qtr = j % 4
                        eoff = 256 * (qtr % 2) + 512 * bank
                        first = (j == 0)
                        last = (j == ntk - 1)
                        for (hs, ebuf, tpos) in ((0, Eb[0], (0, 0)), (64, Eb[1], (0, 64))):
                            w = []
                            if hs == 0:
                                w = [("SC", exp_done[jj])]
                                if first:
                                    w.append(("V", v_norm_done))
                            emit("tensor", (lambda j, hs, ebuf, eoff, tpos, first, last, cch: lambda e: e.matmul(
                                pO[hs:hs + 64, :],
                                vtm[:, D * j + 64 * (2 * cch + hs // 64): D * j + 64 * (2 * cch + hs // 64) + 64],
                                ebuf[:, eoff:eoff + 256],
                                start=first, stop=last,
                                tile_position=tpos))(j, hs, ebuf, eoff, tpos, first, last, cch),
                                waits=w)
                        for (hs, ebuf, tpos) in ((0, Eb[0], (0, 0)), (64, Eb[1], (0, 64))):
                            emit("tensor", (lambda hs, ebuf, eoff, tpos, first, last: lambda e: e.matmul(
                                pl[hs:hs + 64, :], ones_bf[:], ebuf[:, eoff:eoff + 256],
                                start=first, stop=last,
                                tile_position=tpos))(hs, ebuf, eoff, tpos, first, last),
                                sem=("T" if hs == 64 else None))
                    avl_done[jj] = p.val("T")

                qk_done = {}
                exp_done = {}
                avl_done = {}
                for jj in range(nsteps):
                    emit_qk(jj)
                    qk_done[jj] = p.val("T")
                    if jj >= 1:
                        emit_exp(jj - 1)
                    if jj >= 2:
                        emit_avl(jj - 2)
                emit_exp(nsteps - 1)
                if nsteps >= 2:
                    emit_avl(nsteps - 2)
                emit_avl(nsteps - 1)
                emit("scalar", lambda e: e.activation(lnl[:], pl[:], AF.Ln, bias=0.0, scale=1.0),
                     sem="SC", waits=["T"])
                emit("scalar", lambda e: e.activation(recl[:], lnl[:], AF.Exp, bias=0.0, scale=-1.0),
                     sem="SC", waits=["SC"])
                emit("vector", (lambda cch: lambda e: e.tensor_tensor(
                    Ofm[:, 256 * cch:256 * (cch + 1)], pO[:], recl[:], ALU.mult))(cch),
                    sem="V", waits=["SC"])
            if DBGA2:
                emit("gpsimd", (lambda b: lambda e: e.dma_start(
                    dbgo_d[128 * b:128 * (b + 1), :], Ofm[:]))(b), sem="dG", n=16, waits=["V"])
            for m2 in range(2):
                for nn in range(2):
                    for c2 in range(NCH):
                        emit("tensor", (lambda m2, nn, c2: lambda e: e.matmul(
                            pproj[nn][:], Ofm[:, 256 * c2 + 128 * m2: 256 * c2 + 128 * (m2 + 1)],
                            wprojb[:, D * c2 + 512 * nn: D * c2 + 512 * (nn + 1)],
                            start=(c2 == 0), stop=False))(m2, nn, c2),
                            waits=(["V"] if c2 == 0 else []))
                    emit("tensor", (lambda nn: lambda e: e.matmul(
                        pproj[nn][:], onesrow_bf[:], bproj_bf[0:1, 512 * nn:512 * (nn + 1)],
                        start=False, stop=True))(nn), sem="T")
                    emit("vector", (lambda nn: lambda e: e.tensor_copy(
                        attst[nn][:], pproj[nn][:]))(nn), sem="V",
                        waits=["T", "GA%d" % nn])
                    emit("gpsimd", (lambda b, m2, nn: lambda e: e.dma_start(
                        attn_d[256 * b + 128 * m2: 256 * b + 128 * (m2 + 1), 512 * nn:512 * (nn + 1)],
                        attst[nn][:]))(b, m2, nn), sem="GA%d" % nn, n=16, waits=["V"])

    p.barrier()

    if "C" in phases:
      with (
        nc.psum_tensor("pg0", [128, 512], F32) as pg0,
        nc.psum_tensor("pg1", [128, 512], F32) as pg1,
        nc.psum_tensor("pu0", [128, 512], F32) as pu0,
        nc.psum_tensor("pu1", [128, 512], F32) as pu1,
        nc.psum_tensor("pd0", [128, 512], F32) as pd0,
        nc.psum_tensor("pd1", [128, 512], F32) as pd1,
        nc.psum_tensor("ptrh", [128, 1024], BF16) as ptrh,
      ):
        pg = [pg0, pg1]
        pu = [pu0, pu1]
        pd = [pd0, pd1]
        xq_src = xqfm_d[:].rearrange("(c p) s -> p c s", p=128)
        tg_src = terng_d[:].rearrange("(c p) q -> p c q", p=128)
        tu_src = ternu_d[:].rearrange("(c p) q -> p c q", p=128)
        hqfm_w = hqfm[:].rearrange("p (j mm w) -> p j mm w", mm=4, w=128)
        ptrh_r = ptrh[:].rearrange("p (j w) -> p j w", w=128)
        tern_free = [0, 0]
        terndP_free = [0, 0]
        t_gate_done = 0
        x_free = [0, 0]
        o_free = [0, 0]
        hq_done = 0
        for M in range(CN):
            emit("sync", (lambda M: lambda e: e.dma_start(
                xqf[:].rearrange("p (c w) -> p c w", c=NCH),
                xq_src[:, :, 512 * M:512 * (M + 1)]))(M),
                sem="XQ", n=16, waits=[("T", (t_gate_done if M > 0 else p.val("T")))])
            for m in range(4):
                emit("vector", (lambda M, m: lambda e: e.tensor_scalar(
                    hstats[:, 40 + m:41 + m], cx[:, 4 * M + m:4 * M + m + 1],
                    crep[:, 0:1], None, ALU.mult))(M, m), sem="V", waits=["V"])
                emit("vector", (lambda M, m: lambda e: e.tensor_scalar(
                    hstats[:, 44 + m:45 + m], cx[:, 4 * M + m:4 * M + m + 1],
                    crep[:, 2:3], None, ALU.mult))(M, m), sem="V", waits=["V"])
            gu_done = [p.val("V"), p.val("V")]
            for ic in (range(8) if "gu" not in CSKIP else []):
                tpar = ic % 2
                emit("sync", (lambda ic, tpar: lambda e: e.dma_start(
                    terng[tpar][:].rearrange("p (c w) -> p c w", c=NCH),
                    tg_src[:, :, 512 * ic:512 * (ic + 1)]))(ic, tpar),
                    sem="TG%d" % tpar, n=16, waits=[("V", tern_free[tpar])])
                emit("sync", (lambda ic, tpar: lambda e: e.dma_start(
                    ternu[tpar][:].rearrange("p (c w) -> p c w", c=NCH),
                    tu_src[:, :, 512 * ic:512 * (ic + 1)]))(ic, tpar),
                    sem="TG%d" % tpar, n=16)
                for m in range(4):
                    gpar = (ic * 4 + m) % 2
                    for c in range(NCH):
                        emit("tensor", (lambda c, m, gpar, tpar: lambda e: e.matmul(
                            pg[gpar][:], xqf[:, 512 * c + 128 * m: 512 * c + 128 * (m + 1)],
                            terng[tpar][:, 512 * c:512 * (c + 1)],
                            start=(c == 0), stop=(c == NCH - 1)))(c, m, gpar, tpar),
                            waits=(["TG%d" % tpar, "XQ", ("V", gu_done[gpar])] if c == 0 else []))
                    for c in range(NCH):
                        emit("tensor", (lambda c, m, gpar, tpar: lambda e: e.matmul(
                            pu[gpar][:], xqf[:, 512 * c + 128 * m: 512 * c + 128 * (m + 1)],
                            ternu[tpar][:, 512 * c:512 * (c + 1)],
                            start=(c == 0), stop=(c == NCH - 1)))(c, m, gpar, tpar),
                            sem=("T" if c == NCH - 1 else None))
                    emit("scalar", (lambda m, gpar: lambda e: e.activation(
                        gbuf[:], pg[gpar][:], AF.Silu, bias=0.0,
                        scale=hstats[:, 40 + m:41 + m]))(m, gpar), sem="SC", waits=["T", "V"])
                    emit("vector", (lambda m, ic, gpar: lambda e: e.tensor_tensor(
                        hB[:, 4096 * m + 512 * ic: 4096 * m + 512 * (ic + 1)], gbuf[:],
                        pu[gpar][:], ALU.mult))(m, ic, gpar), waits=["SC"])
                    emit("vector", (lambda m, ic: lambda e: e.tensor_reduce(
                        hstats[:, 8 * m + ic: 8 * m + ic + 1],
                        hB[:, 4096 * m + 512 * ic: 4096 * m + 512 * (ic + 1)], AX, ALU.max,
                        apply_absolute_value=True))(m, ic), sem="V")
                    gu_done[gpar] = p.val("V")
                tern_free[tpar] = p.val("V")
            t_gate_done = p.val("T")

            def emit_tail(m):
                nonlocal hq_done
                if "tail" in CSKIP:
                    return
                emit("vector", (lambda m: lambda e: e.tensor_reduce(
                    hstats[:, 32 + m:33 + m], hstats[:, 8 * m:8 * m + 8], AX, ALU.max))(m),
                    sem="V", waits=["V"])
                emit("vector", (lambda m: lambda e: e.tensor_tensor(
                    hstats[:, 36:37], hstats[:, 32 + m:33 + m], hstats[:, 44 + m:45 + m],
                    ALU.mult))(m), sem="V", waits=["V"])
                emit("vector", lambda e: e.tensor_scalar(hstats[:, 37:38], hstats[:, 36:37], 1e-5, None, ALU.max),
                     sem="V", waits=["V"])
                emit("vector", lambda e: e.reciprocal(hstats[:, 38:39], hstats[:, 37:38]),
                     sem="V", waits=["V"])
                emit("vector", (lambda m: lambda e: e.tensor_tensor(
                    hstats[:, 36:37], hstats[:, 38:39], hstats[:, 44 + m:45 + m], ALU.mult))(m),
                    sem="V", waits=["V"])
                emit("vector", lambda e: e.tensor_scalar(hstats[:, 39:40], hstats[:, 36:37], 127.0, None, ALU.mult),
                     sem="V", waits=["V"])
                emit("vector", (lambda m: lambda e: e.tensor_scalar(
                    hstats[:, 48 + m:49 + m], hstats[:, 37:38], 1.0 / 127.0, None, ALU.mult))(m),
                    sem="V", waits=["V"])
                emit("vector", (lambda m: lambda e: e.tensor_scalar(
                    hstats[:, 48 + m:49 + m], hstats[:, 48 + m:49 + m], crep[:, 4:5], None, ALU.mult))(m),
                    sem="V", waits=["V"])
                for q in range(8):
                    qp = q % 2
                    emit("vector", (lambda m, q, qp: lambda e: e.tensor_scalar(
                        hqi[qp][:], hB[:, 4096 * m + 512 * q: 4096 * m + 512 * (q + 1)],
                        hstats[:, 39:40], None, ALU.mult))(m, q, qp),
                        waits=(["V"] if q == 0 else []))
                    emit("vector", (lambda q, qp: lambda e: e.tensor_scalar(
                        hqbf[:, 512 * q:512 * (q + 1)], hqi[qp][:], 1.0, None, ALU.mult))(q, qp),
                        sem="V")
                for grp in range(4):
                    for i2 in range(8):
                        emit("tensor", (lambda grp, i2: lambda e: e.transpose(
                            ptrh[:, 128 * i2:128 * (i2 + 1)],
                            hqbf[:, 1024 * grp + 128 * i2: 1024 * grp + 128 * (i2 + 1)], ident[:]))(grp, i2),
                            waits=((["V", ("T", hq_done)] if grp == 0 else ["V"]) if i2 == 0 else []),
                            sem=("T" if i2 == 7 else None))
                    emit("vector", (lambda grp, m: lambda e: e.tensor_copy(
                        hqfm_w[:, 8 * grp:8 * (grp + 1), m, :], ptrh_r[:]))(grp, m),
                        sem="V", waits=["T"])
                hq_done = p.val("T")
                if DBGC:
                    emit("gpsimd", (lambda M, m: lambda e: e.dma_start(
                        dbghq_d[512 * M + 128 * m: 512 * M + 128 * (m + 1), :], hqbf[:]))(M, m),
                        sem="dG", n=16, waits=["V"])
                    emit("gpsimd", (lambda M, m: lambda e: e.dma_start(
                        dbghst_d[128 * (4 * M + m): 128 * (4 * M + m + 1), :], hstats[:]))(M, m),
                        sem="dG", n=16)
                    emit("gpsimd", (lambda M, m: lambda e: e.dma_start(
                        dbghqf_d[128 * (4 * M + m): 128 * (4 * M + m + 1), :], hqfm[:]))(M, m),
                        sem="dG", n=16, waits=["V"])
            emit_tail(0)
            emit_tail(1)
            sc_now = p.val("SC")
            tdsrc = ternd_d[:].rearrange("(g p) q -> p g q", p=128)
            for half in ((0, 1) if ("tail" not in CSKIP and "down" not in CSKIP) else ()):
                v_hq_done = p.val("V")
                pdq = [[pd0, pd1], [pg0, pg1]]
                for j2 in range(16):
                    if half == 0 and j2 == 1:
                        emit_tail(2)
                    if half == 0 and j2 == 8:
                        emit_tail(3)
                    jp = j2 % 2
                    emit("sync", (lambda j2, jp: lambda e: e.dma_start(
                        terndP[jp][:].rearrange("p (g q) -> p g q", g=2),
                        tdsrc[:, 2 * j2:2 * j2 + 2, :]))(j2, jp),
                        sem="TD%d" % jp, n=16, waits=[("T", terndP_free[jp])])
                    for jsub in (0, 1):
                        j = 2 * j2 + jsub
                        for m2 in (0, 1):
                            m = 2 * half + m2
                            for nn in range(2):
                                w = []
                                if jsub == 0 and m2 == 0 and nn == 0:
                                    w = ["TD%d" % jp]
                                    if j2 == 0:
                                        w += [("V", v_hq_done), ("SC", sc_now)]
                                emit("tensor", (lambda j, jp, jsub, m, m2, nn: lambda e: e.matmul(
                                    pdq[m2][nn][:], hqfm[:, 512 * j + 128 * m: 512 * j + 128 * (m + 1)],
                                    terndP[jp][:, 1024 * jsub + 512 * nn: 1024 * jsub + 512 * (nn + 1)],
                                    start=(j == 0), stop=(j == 31)))(j, jp, jsub, m, m2, nn),
                                    waits=w,
                                    sem=("T" if (jsub == 1 and m2 == 1 and nn == 1) else None))
                    terndP_free[jp] = p.val("T")
                for m2 in (0, 1):
                    m = 2 * half + m2
                    opar = m % 2
                    for nn in range(2):
                        emit("vector", (lambda m, m2, nn: lambda e: e.tensor_scalar(
                            dbuf[:, 512 * nn:512 * (nn + 1)], pdq[m2][nn][:], hstats[:, 48 + m:49 + m], None,
                            ALU.mult))(m, m2, nn), sem="V", waits=["T", "V"])
                    emit("sync", (lambda M, m, opar: lambda e: e.dma_start(
                        xrel[opar][:], x_d[512 * M + 128 * m: 512 * M + 128 * (m + 1), :]))(M, m, opar),
                        sem="XR", n=16, waits=[("V", x_free[opar])])
                    emit("sync", (lambda M, m, opar: lambda e: e.dma_start(
                        attb[opar][:], attn_d[512 * M + 128 * m: 512 * M + 128 * (m + 1), :]))(M, m, opar),
                        sem="XR", n=16)
                    emit("vector", (lambda opar: lambda e: e.tensor_add(
                        obuf[opar][:], dbuf[:], xrel[opar][:]))(opar),
                        waits=["XR"])
                    emit("vector", (lambda opar: lambda e: e.tensor_add(
                        obuf2[opar][:], obuf[opar][:], attb[opar][:]))(opar), sem="V",
                        waits=["GO"])
                    x_free[0] = p.val("V")
                    x_free[1] = x_free[0]
                    emit("gpsimd", (lambda M, m, opar: lambda e: e.dma_start(
                        out_d[512 * M + 128 * m: 512 * M + 128 * (m + 1), :], obuf2[opar][:]))(M, m, opar),
                        sem="GO", n=16, waits=["V"])
            gu_done_next = p.val("V")
    final_waits = [(s, p.val(s)) for s in SEMNAMES
                   if s not in ("T", "V", "SC", "GP") and p.val(s) > 0]

    SEMS = {}
    sem_guards = []
    for s in SEMNAMES:
        gctx = nc.semaphore(s)
        SEMS[s] = gctx.__enter__()
        sem_guards.append(gctx)
    p.install(nc, SEMS, final_waits)
    for gctx in reversed(sem_guards):
        gctx.__exit__(None, None, None)
    return nc


_NC_CACHE = None


def kernel(L0_attn_x, L0_attn_w_qkv, L0_attn_w_proj, L0_attn_b_proj,
           L0_ffn_w_gate, L0_ffn_w_up, L0_ffn_w_down):
    global _NC_CACHE
    if _NC_CACHE is None:
        _NC_CACHE = build_program()
    nc = _NC_CACHE

    x = np.ascontiguousarray(np.asarray(L0_attn_x, dtype=np.float32))
    wqkv = np.asarray(L0_attn_w_qkv, dtype=np.float32)
    wqkv = np.ascontiguousarray(
        wqkv.reshape(8, 128, 24, 128).transpose(2, 1, 0, 3).reshape(3072, 1024))
    wproj = np.ascontiguousarray(np.asarray(L0_attn_w_proj, dtype=np.float32))
    bproj = np.ascontiguousarray(np.asarray(L0_attn_b_proj, dtype=np.float32)).reshape(1, D)
    wgT = np.ascontiguousarray(np.asarray(L0_ffn_w_gate, dtype=np.float32).T)
    wuT = np.ascontiguousarray(np.asarray(L0_ffn_w_up, dtype=np.float32).T)
    wdT = np.ascontiguousarray(np.asarray(L0_ffn_w_down, dtype=np.float32).T)

    iv64 = 1.0 / (10000.0 ** (np.arange(0, 64, 2, dtype=np.float64) / 64.0))
    invf_hi = iv64.astype(np.float32).reshape(32, 1)
    invf_lo = (iv64 - invf_hi.astype(np.float64).reshape(-1)).astype(np.float32).reshape(32, 1)

    shared = {
        "wqkv": wqkv, "wproj": wproj, "bproj": bproj,
        "wgT": wgT, "wuT": wuT, "wdT": wdT,
        "invf_hi": invf_hi, "invf_lo": invf_lo,
    }
    in_maps = [dict(shared, x=np.ascontiguousarray(x[b])) for b in range(8)]
    res = run_bass_kernel_spmd(nc, in_maps, core_ids=list(range(8)))
    out = np.stack([np.asarray(res.results[b]["out"]) for b in range(8)], axis=0)
    return out.astype(np.float32)
